# version 14
# speedup vs baseline: 1.8703x; 1.0141x over previous
import math
import numpy as np
import ml_dtypes
import concourse.bass as bass
import concourse.mybir as mybir
from concourse.bass_utils import run_bass_kernel_spmd

F32 = mybir.dt.float32
BF16 = mybir.dt.bfloat16
AF = mybir.ActivationFunctionType
ALU = mybir.AluOpType
AX = mybir.AxisListType

T = 2048
D = 1024
NH = 16
DH = 64
NSEQ = 2
NTOK = NSEQ * T
KC = 8
TC = T // 128
NT = T // 512
B_EXP = 12.0


class Sem:
    def __init__(self, nc, name):
        self.h = nc.alloc_semaphore(name)
        self.n = 0

    def inc(self, step=1):
        self.n += step
        return self.n


def build_nc(debug=False):
    nc = bass.Bass("TRN2", target_bir_lowering=False,
                   detect_race_conditions=False)
    ikind = "ExternalOutput" if debug else "Internal"

    xin = nc.dram_tensor("xin", [NTOK, D], F32, kind="ExternalInput").ap()
    wqk = nc.dram_tensor("wqk", [16, 128, KC, 128], F32, kind="ExternalInput").ap()
    wvt = nc.dram_tensor("wvt", [2, 128, KC, 512], F32, kind="ExternalInput").ap()
    wot = nc.dram_tensor("wot", [KC, 128, KC, 128], F32, kind="ExternalInput").ap()
    wft = nc.dram_tensor("wft", [KC, 128, D], F32, kind="ExternalInput").ap()
    hmt = nc.dram_tensor("hmt", [2, 128, KC, 512], F32, kind="ExternalInput").ap()
    iota = nc.dram_tensor("iota", [128, T], F32, kind="ExternalInput").ap()
    thet = nc.dram_tensor("thet", [128, 12], F32, kind="ExternalInput").ap()
    out = nc.dram_tensor("out", [NTOK, D], F32, kind="ExternalOutput").ap()

    qd = nc.dram_tensor("qd", [2, KC, 128, T], BF16, kind=ikind).ap()
    kd = nc.dram_tensor("kd", [2, KC, 128, T], BF16, kind=ikind).ap()
    vd = nc.dram_tensor("vd", [128, TC, D], F32, kind=ikind).ap()
    yd = nc.dram_tensor("yd", [128, KC, T], F32, kind=ikind).ap()
    ad = nc.dram_tensor("ad", [128, KC, T], F32, kind=ikind).ap()
    fd = nc.dram_tensor("fd", [128, KC, T], F32, kind=ikind).ap()
    xqd = nc.dram_tensor("xqd", [128, KC, T], BF16, kind=ikind).ap()
    ternd = nc.dram_tensor("ternd", [128, KC, D], BF16, kind=ikind).ap()
    if debug:
        xnd_dbg = nc.dram_tensor("xnd_dbg", [128, KC * T], F32,
                                 kind="ExternalOutput").ap()
        st_dbg = nc.dram_tensor("st_dbg", [128, 64], F32,
                                kind="ExternalOutput").ap()
        sb_dbg = nc.dram_tensor("sb_dbg", [128, KC + TC], F32,
                                kind="ExternalOutput").ap()

    A = nc.alloc_sbuf_tensor
    XN = A("XN", [128, KC * T], F32).ap()
    YB = A("YB", [128, KC * D], F32).ap()
    WBv = A("WBv", [128, KC * 512], F32).ap()

    ctab = A("ctab", [128, T], F32).ap()
    stab = A("stab", [128, T], F32).ap()
    so_sb = A("so_sb", [128, KC], F32).ap()
    sc_sb = A("sc_sb", [128, TC], F32).ap()
    trimask = A("trimask", [128, 128], F32).ap()
    identf = A("identf", [128, 128], F32).ap()
    identb = A("identb", [128, 128], BF16).ap()
    ones64 = A("ones64", [128, 64], F32).ap()
    cst = A("cst", [128, 12], F32).ap()
    stats = A("stats", [128, 64], F32).ap()
    dum = A("dum", [128, 8], F32).ap()
    dumw = A("dumw", [128, 384], F32).ap()

    wdb = A("wdb", [128, KC * 128], F32).ap()
    xdb = A("xdb", [128, D], F32).ap()
    stg = A("stg", [128, 2 * 512], F32).ap()
    yst = stg[0:64, 0:512]
    qha = A("qha", [128, T], BF16).ap()
    qlb = A("qlb", [128, T], BF16).ap()
    khl = A("khl", [128, T], BF16).ap()
    vh = A("vh", [128, TC * DH], F32).ap()
    eb = A("eb", [128, 2 * 512], F32).ap()
    zsb = A("zsb", [64, 512], F32).ap()
    ystr = A("ystr", [64, 512], F32).ap()
    scr1 = A("scr1", [128, D], F32).ap()
    scr2 = A("scr2", [128, D], F32).ap()
    xqtm = A("xqtm", [128, D], BF16).ap()
    obuf = A("obuf", [128, D], F32).ap()
    fstg = A("fstg", [128, 2 * 512], F32).ap()
    xqb = A("xqb", [128, KC * 512], BF16).ap()
    ntb = A("ntb", [128, 2 * KC * 128], BF16).ap()
    xqstg = A("xqstg", [128, 2 * 512], BF16).ap()
    ntstg = A("ntstg", [128, KC * 128], BF16).ap()

    qha2 = xqb[:, 0:T]
    qlb2 = xqb[:, T:2 * T]
    khl2 = ntb[:, 0:T]
    vh2 = obuf

    def ebsl(eslot, j, end=512):
        if eslot < 2:
            base = eslot * 512
            return eb[:, base + j:base + end]
        base = (eslot - 2) * 512
        return fstg[:, base + j:base + end]

    P = nc.alloc_psum_tensor
    ps = [P(f"ps{i}", [128, 512], F32).ap() for i in range(6)]
    psY = P("psY", [64, 512], F32).ap()
    psZ = P("psZ", [64, 512], F32).ap()

    ops = {e: [] for e in ("sync", "gpsimd", "tensor", "vector", "scalar")}

    def on(eng, fn):
        ops[eng].append(fn)

    sems = {}

    def sem(name):
        if name not in sems:
            sems[name] = Sem(nc, name)
        return sems[name]

    tpi = 2.0 * math.pi
    c2ph = float(np.float32(tpi)); c2pl = float(np.float32(tpi - float(np.float32(tpi))))
    cph = float(np.float32(math.pi)); cpl = float(np.float32(math.pi - float(np.float32(math.pi))))
    i2p = 1.0 / tpi
    ci2h = float(np.float32(i2p)); ci2l = float(np.float32(i2p - float(np.float32(i2p))))
    p2 = math.pi / 2
    cp2h = float(np.float32(p2)); cp2l = float(np.float32(p2 - float(np.float32(p2))))
    CLAMP = 3.1415925
    C7 = float(np.float32(1.0 / 7.0))
    INV1024 = float(np.float32(2.0 ** -10))

    idf_np = np.eye(128, dtype=np.float32)
    idb_np = np.eye(128, dtype=ml_dtypes.bfloat16).view(np.uint16)
    tri_np = (np.arange(128)[:, None] <= np.arange(128)[None, :]).astype(np.float32)
    t_idf = nc.inline_tensor(idf_np, "idf")
    t_idb = nc.inline_tensor(idb_np, "idb")
    t_tri = nc.inline_tensor(tri_np, "tri")

    dTB = sem("dTB"); vTB = sem("vTB"); sTB = sem("sTB"); tTB = sem("tTB")
    dTN = sem("dTN")

    def tb_load(e):
        e.dma_start(out=ctab[:], in_=iota).then_inc(dTB.h, 16)
        e.dma_start(out=cst[:], in_=thet).then_inc(dTB.h, 16)
        e.dma_start(out=identf[:], in_=t_idf.ap()).then_inc(dTB.h, 16)
        e.dma_start(out=identb[:], in_=t_idb.ap().bitcast(BF16)).then_inc(dTB.h, 16)
        e.dma_start(out=trimask[:], in_=t_tri.ap()).then_inc(dTB.h, 16)
    on("sync", tb_load)
    dTB.inc(16 * 5)
    n_tbload = dTB.n

    ANG = YB[:, 0:T]
    WRK = YB[:, T:2 * T]
    WK2 = YB[:, 2 * T:3 * T]
    RR = YB[:, 3 * T:4 * T]

    for is_cos in (False, True):
        def tab_v1(e, is_cos=is_cos, w=n_tbload):
            e.wait_ge(dTB.h, w)
            e.tensor_scalar(out=ANG, in0=ctab[:], scalar1=cst[:, 0:1],
                            scalar2=None, op0=ALU.mult)
            e.tensor_scalar(out=WRK, in0=ctab[:], scalar1=cst[:, 1:2],
                            scalar2=None, op0=ALU.mult)
            e.tensor_tensor(out=ANG, in0=ANG, in1=WRK, op=ALU.add)
            if is_cos:
                e.tensor_scalar(out=ANG, in0=ANG, scalar1=cp2h, scalar2=cp2l,
                                op0=ALU.add, op1=ALU.add)
            e.tensor_scalar(out=WRK, in0=ANG, scalar1=ci2h, scalar2=None,
                            op0=ALU.mult)
            e.tensor_scalar(out=WK2, in0=ANG, scalar1=ci2l, scalar2=None,
                            op0=ALU.mult)
            e.tensor_tensor(out=WRK, in0=WRK, in1=WK2, op=ALU.add)
            e.tensor_scalar(out=WRK, in0=WRK, scalar1=0.5, scalar2=None,
                            op0=ALU.add)
            e.tensor_copy(out=WK2, in_=WRK)
            for lv in (256., 128., 64., 32., 16., 8., 4., 2., 1.):
                e.tensor_scalar(out=RR, in0=WRK, scalar1=lv, scalar2=-lv,
                                op0=ALU.is_ge, op1=ALU.mult)
                e.tensor_tensor(out=WRK, in0=WRK, in1=RR, op=ALU.add)
            e.tensor_tensor(out=WK2, in0=WK2, in1=WRK, op=ALU.subtract)
            e.scalar_tensor_tensor(out=RR, in0=WK2, scalar=-c2ph, in1=ANG,
                                   op0=ALU.mult, op1=ALU.add)
            e.scalar_tensor_tensor(out=WRK, in0=WK2, scalar=-c2pl, in1=RR,
                                   op0=ALU.mult, op1=ALU.add)
            e.tensor_scalar(out=WRK, in0=WRK, scalar1=CLAMP, scalar2=-CLAMP,
                            op0=ALU.min, op1=ALU.max).then_inc(vTB.h, 1)
        on("vector", tab_v1)
        vTB.inc()
        nv = vTB.n

        def tab_s(e, nv=nv):
            e.wait_ge(vTB.h, nv)
            e.activation(out=RR, in_=WRK, func=AF.Sin).then_inc(sTB.h, 1)
        on("scalar", tab_s)
        sTB.inc()
        ns = sTB.n

        def tab_v2(e, is_cos=is_cos, ns=ns):
            e.wait_ge(sTB.h, ns)
            if is_cos:
                e.tensor_tensor(out=ctab[:], in0=RR, in1=RR,
                                op=ALU.max).then_inc(vTB.h, 1)
            else:
                e.tensor_scalar(out=stab[:], in0=RR, scalar1=cst[:, 2:3],
                                scalar2=None, op0=ALU.mult).then_inc(vTB.h, 1)
        on("vector", tab_v2)
        vTB.inc()

    def tb_consts(e):
        e.memset(ones64[:], 1.0)
        e.memset(dumw[:], 0.0)
        e.memset(stats[:], 0.0)
        e.memset(dum[:], 0.0)
        e.memset(sc_sb[:], 0.0)
        e.memset(so_sb[:], 0.0)
        e.memset(qlb[64:128, :], 0.0)
    on("vector", tb_consts)

    for oc in range(KC):
        w_t = tTB.n

        def f_sy(e, oc=oc, w_t=w_t):
            if oc > 0:
                e.wait_ge(tTB.h, w_t)
            e.dma_start(out=WBv[:, 0:D], in_=wft[oc]).then_inc(dTB.h, 16)
        on("sync", f_sy)
        dTB.inc(16)
        n_wf = dTB.n

        def f_v(e, oc=oc, n_wf=n_wf):
            e.wait_ge(dTB.h, n_wf)
            wt_ = WBv[:, 0:D]
            i0 = e.tensor_reduce(out=stats[:, 16:17], in_=wt_, axis=AX.X,
                                 op=ALU.add, apply_absolute_value=True)
            vbar(e, i0)
            i1 = e.tensor_tensor(out=stats[:, 16:17], in0=stats[:, 16:17],
                                 in1=cst[:, 5:6], op=ALU.mult)
            vbar(e, i1)
            i2 = e.tensor_tensor(out=stats[:, 16:17], in0=stats[:, 16:17],
                                 in1=cst[:, 6:7], op=ALU.max)
            vbar(e, i2)
            e.tensor_tensor(out=so_sb[:, oc:oc + 1], in0=stats[:, 16:17],
                            in1=stats[:, 16:17], op=ALU.max)
            i3 = e.tensor_tensor(out=stats[:, 20:21], in0=stats[:, 16:17],
                                 in1=cst[:, 7:8], op=ALU.mult)
            vbar(e, i3)
            i4 = e.tensor_tensor(out=stats[:, 24:25], in0=stats[:, 20:21],
                                 in1=cst[:, 8:9], op=ALU.mult)
            vbar(e, i4)
            e.tensor_scalar(out=scr1[:], in0=wt_, scalar1=stats[:, 20:21],
                            scalar2=None, op0=ALU.is_gt)
            e.tensor_scalar(out=scr2[:], in0=wt_, scalar1=stats[:, 24:25],
                            scalar2=None, op0=ALU.is_lt)
            e.tensor_tensor(out=xqtm[:], in0=scr2[:], in1=scr1[:],
                            op=ALU.subtract).then_inc(vTB.h, 1)
        on("vector", f_v)
        vTB.inc()
        n_vt = vTB.n
        w_s = sTB.n

        def f_t(e, n_vt=n_vt, w_s=w_s, oc=oc):
            e.wait_ge(vTB.h, n_vt)
            if oc > 0:
                e.wait_ge(sTB.h, w_s)
            last = None
            for j in range(KC):
                dst = ps[j // 4][:, (j % 4) * 128:(j % 4) * 128 + 128]
                last = e.matmul(dst, xqtm[:, j * 128:(j + 1) * 128], identb[:],
                                start=True, stop=True)
            last.then_inc(tTB.h, 1)
        on("tensor", f_t)
        tTB.inc()
        n_tt = tTB.n
        w_dn = dTN.n

        def f_s(e, oc=oc, n_tt=n_tt, w_dn=w_dn):
            e.wait_ge(tTB.h, n_tt)
            if w_dn:
                e.wait_ge(dTN.h, w_dn)
            last = None
            for j in range(KC):
                src = ps[j // 4][:, (j % 4) * 128:(j % 4) * 128 + 128]
                last = e.copy(out=ntstg[:, j * 128:(j + 1) * 128], in_=src)
            last.then_inc(sTB.h, 1)
        on("scalar", f_s)
        sTB.inc()
        n_fs = sTB.n

        def f_g(e, oc=oc, n_fs=n_fs):
            e.wait_ge(sTB.h, n_fs)
            nv_ = ntstg.rearrange("p (k c) -> p k c", k=KC)
            e.dma_start(out=ternd[:, 0:KC, oc * 128:oc * 128 + 128],
                        in_=nv_).then_inc(dTN.h, 16)
        on("gpsimd", f_g)
        dTN.inc(16)

    TBv, TBs, TBt, TBd = vTB.n, sTB.n, tTB.n, dTB.n

    dX = sem("dX"); dW = sem("dW"); dSp = sem("dSp"); dH4 = sem("dH4")
    dY6 = sem("dY6"); dAF = sem("dAF"); dOut = sem("dOut"); dHM = sem("dHM")
    dXQ = sem("dXQ"); dNT = sem("dNT"); dXB = sem("dXB")
    dRL = sem("dRL")
    tP1 = sem("tP1"); tQK = sem("tQK"); tV = sem("tV"); tS = sem("tS")
    tZ = sem("tZ"); tWo = sem("tWo"); tH = sem("tH"); tXq = sem("tXq")
    tTr = sem("tTr"); tFf = sem("tFf")
    sP1a = sem("sP1a"); sP1b = sem("sP1b"); sP1c = sem("sP1c")
    sP3 = sem("sP3"); sV3 = sem("sV3"); sExp = sem("sExp"); sZ = sem("sZ")
    sWo = sem("sWo"); sQ7 = sem("sQ7"); sXq = sem("sXq"); sTr = sem("sTr")
    sFf = sem("sFf")
    vLN1 = sem("vLN1"); vLN2 = sem("vLN2"); vR3 = sem("vR3"); vMk = sem("vMk")
    vY = sem("vY"); vQ7 = sem("vQ7"); vL7 = sem("vL7"); vO = sem("vO")

    vB = sem("vB")

    def vbar(e, inst):
        inst.then_inc(vB.h, 1)
        e.wait_ge(vB.h, vB.inc())

    def vguard(e, semh):
        e.tensor_tensor(out=dumw[:], in0=dumw[:], in1=dumw[:],
                        op=ALU.max).then_inc(semh, 1)

    sdmw = A("sdmw", [128, 448], F32).ap()

    def sguard(e, semh):
        e.copy(out=sdmw[:, 0:384], in_=dumw[:]).then_inc(semh, 1)

    for s in range(NSEQ):
        row0 = s * T

        for tc in range(TC):
            def p1_sy(e, tc=tc, s=s, w=sP1b.n, row0=row0, w_vo=vO.n):
                if s > 0 or tc > 0:
                    e.wait_ge(sP1b.h, w)
                if s > 0:
                    e.wait_ge(vO.h, w_vo)
                e.dma_start(out=xdb[:],
                            in_=xin[row0 + tc * 128: row0 + tc * 128 + 128, :]
                            ).then_inc(dX.h, 16)
            on("sync", p1_sy)
            dX.inc(16)
            n_dx = dX.n

            def p1_v1(e, n_dx=n_dx, w=TBv, w_sq7=sQ7.n):
                e.wait_ge(dX.h, n_dx)
                e.wait_ge(vTB.h, w)
                if w_sq7:
                    e.wait_ge(sQ7.h, w_sq7)
                i0 = e.tensor_reduce(out=stats[:, 16:17], in_=xdb[:],
                                     axis=AX.X, op=ALU.add)
                vbar(e, i0)
                e.tensor_tensor(out=stats[:, 20:21], in0=stats[:, 16:17],
                                in1=cst[:, 5:6],
                                op=ALU.mult).then_inc(vLN1.h, 1)
            on("vector", p1_v1)
            vLN1.inc()
            nv1 = vLN1.n

            def p1_s1(e, n_dx=n_dx, nv1=nv1, w=TBs, w_ttr=tTr.n):
                e.wait_ge(dX.h, n_dx)
                e.wait_ge(sTB.h, w)
                if w_ttr:
                    e.wait_ge(tTr.h, w_ttr)
                e.activation(out=scr2[:], in_=xdb[:], func=AF.Square,
                             accum_out=stats[:, 24:25])
                sguard(e, sP1a.h)
            on("scalar", p1_s1)
            sP1a.inc()
            ns1 = sP1a.n

            def p1_v2(e, nv1=nv1, ns1=ns1):
                e.wait_ge(sP1a.h, ns1)
                e.wait_ge(vLN1.h, nv1)
                e.tensor_tensor(out=stats[:, 28:29], in0=stats[:, 20:21],
                                in1=stats[:, 20:21], op=ALU.mult)
                i1 = e.tensor_tensor(out=stats[:, 32:33], in0=stats[:, 24:25],
                                     in1=cst[:, 5:6], op=ALU.mult)
                vbar(e, i1)
                i2 = e.tensor_tensor(out=stats[:, 32:33], in0=stats[:, 32:33],
                                     in1=stats[:, 28:29], op=ALU.subtract)
                vbar(e, i2)
                e.tensor_tensor(out=stats[:, 32:33], in0=stats[:, 32:33],
                                in1=cst[:, 6:7], op=ALU.add)
                vguard(e, vLN1.h)
            on("vector", p1_v2)
            vLN1.inc()
            nv2 = vLN1.n

            def p1_s2(e, nv2=nv2):
                e.wait_ge(vLN1.h, nv2)
                e.activation(out=stats[:, 36:37], in_=stats[:, 32:33],
                             func=AF.Sqrt)
                sguard(e, sP1a.h)
            on("scalar", p1_s2)
            sP1a.inc()
            ns2 = sP1a.n

            def p1_v3(e, ns2=ns2):
                e.wait_ge(sP1a.h, ns2)
                i0 = e.reciprocal(out=stats[:, 40:41], in_=stats[:, 36:37])
                vbar(e, i0)
                i1 = e.tensor_tensor(out=stats[:, 44:45], in0=stats[:, 32:33],
                                     in1=stats[:, 40:41], op=ALU.mult)
                vbar(e, i1)
                i2 = e.tensor_tensor(out=stats[:, 44:45], in0=stats[:, 36:37],
                                     in1=stats[:, 44:45], op=ALU.add)
                vbar(e, i2)
                i3 = e.tensor_tensor(out=stats[:, 44:45], in0=stats[:, 44:45],
                                     in1=cst[:, 7:8], op=ALU.mult)
                vbar(e, i3)
                i4 = e.reciprocal(out=stats[:, 48:49], in_=stats[:, 44:45])
                vbar(e, i4)
                i5 = e.tensor_tensor(out=stats[:, 52:53], in0=stats[:, 20:21],
                                     in1=stats[:, 48:49], op=ALU.mult)
                vbar(e, i5)
                e.tensor_tensor(out=stats[:, 52:53], in0=stats[:, 52:53],
                                in1=cst[:, 8:9], op=ALU.mult)
                vguard(e, vLN2.h)
            on("vector", p1_v3)
            vLN2.inc()
            nv3 = vLN2.n
            w_tp1 = tP1.n

            def p1_s3(e, nv3=nv3, w_tp1=w_tp1, w_ttr=tTr.n, w_do=dOut.n):
                e.wait_ge(vLN2.h, nv3)
                if w_tp1:
                    e.wait_ge(tP1.h, w_tp1)
                if w_ttr:
                    e.wait_ge(tTr.h, w_ttr)
                if w_do:
                    e.wait_ge(dOut.h, w_do)
                e.activation(out=scr1[:], in_=xdb[:], func=AF.Identity,
                             scale=stats[:, 48:49],
                             bias=stats[:, 52:53]).then_inc(sP1b.h, 1)
            on("scalar", p1_s3)
            sP1b.inc()
            ns3 = sP1b.n
            w_ev = sP1c.n

            def p1_t(e, ns3=ns3, w_ev=w_ev, w_tb=TBs):
                e.wait_ge(sP1b.h, ns3)
                e.wait_ge(sTB.h, w_tb)
                if w_ev:
                    e.wait_ge(sP1c.h, w_ev)
                last = None
                for j in range(KC):
                    dst = ps[j // 4][:, (j % 4) * 128:(j % 4) * 128 + 128]
                    last = e.matmul(dst, scr1[:, j * 128:(j + 1) * 128],
                                    identf[:], start=True, stop=True)
                last.then_inc(tP1.h, 1)
            on("tensor", p1_t)
            tP1.inc()
            nt1 = tP1.n

            def p1_s4(e, nt1=nt1, tc=tc):
                e.wait_ge(tP1.h, nt1)
                xv = XN.rearrange("p (k t) -> p k t", k=KC)
                e.copy(out=xv[:, 0:4, tc * 128:tc * 128 + 128],
                       in_=ps[0].rearrange("p (k t) -> p k t", k=4))
                e.copy(out=xv[:, 4:8, tc * 128:tc * 128 + 128],
                       in_=ps[1].rearrange("p (k t) -> p k t", k=4)
                       ).then_inc(sP1c.h, 1)
            on("scalar", p1_s4)
            sP1c.inc()

        if debug and s == 0:
            def dbg_dump(e, w_xn=sP1c.n):
                e.wait_ge(sP1c.h, w_xn)
                e.dma_start(out=xnd_dbg, in_=XN[:]).then_inc(dSp.h, 16)
                e.dma_start(out=st_dbg, in_=stats[:]).then_inc(dSp.h, 16)
                e.dma_start(out=sb_dbg[:, 0:KC], in_=so_sb[:]).then_inc(dSp.h, 16)
                e.dma_start(out=sb_dbg[:, KC:KC + TC], in_=sc_sb[:]).then_inc(dSp.h, 16)
            on("gpsimd", dbg_dump)
            dSp.inc(64)

        qk_mm = {}
        for mc in range(16):
            wsel = wdb if mc % 2 == 0 else eb

            def p3_syw(e, mc=mc, wsel=wsel, w_qk=qk_mm.get(mc - 2, 0),
                       w_wo=tWo.n):
                if w_qk:
                    e.wait_ge(tQK.h, w_qk)
                if w_wo:
                    e.wait_ge(tWo.h, w_wo)
                e.dma_start(out=wsel[:], in_=wqk[mc]).then_inc(dW.h, 16)
            on("sync", p3_syw)
            dW.inc(16)
            n_dw = dW.n

            for nt in range(NT):
                idx = mc * NT + nt
                b = 2 + idx % 2
                w_ps = sP3.n - 1
                w_xn = sP1c.n

                def p3_t(e, nt=nt, b=b, wsel=wsel, n_dw=n_dw, w_ps=w_ps,
                         w_xn=w_xn):
                    e.wait_ge(dW.h, n_dw)
                    e.wait_ge(sP1c.h, w_xn)
                    if w_ps >= 1:
                        e.wait_ge(sP3.h, w_ps)
                    xv = XN.rearrange("p (k t) -> p k t", k=KC)
                    wv_ = wsel.rearrange("p (k c) -> p k c", k=KC)
                    last = None
                    for k in range(KC):
                        last = e.matmul(ps[b][:], wv_[:, k, :],
                                        xv[:, k, nt * 512:(nt + 1) * 512],
                                        start=(k == 0), stop=(k == KC - 1))
                    last.then_inc(tQK.h, 1)
                on("tensor", p3_t)
                tQK.inc()
                ntq = tQK.n
                qk_mm[mc] = tQK.n
                w_stg = dSp.n

                def p3_s(e, ntq=ntq, b=b, w_stg=w_stg):
                    e.wait_ge(tQK.h, ntq)
                    if w_stg:
                        e.wait_ge(dSp.h, w_stg)
                    e.copy(out=stg[:, 0:512], in_=ps[b][:]).then_inc(sP3.h, 1)
                on("scalar", p3_s)
                sP3.inc()
                ns_ = sP3.n

                def p3_v(e, ns_=ns_, nt=nt):
                    e.wait_ge(sP3.h, ns_)
                    cs = slice(nt * 512, (nt + 1) * 512)
                    e.stream_shuffle(out=stg[:, 512:1024], in_=stg[:, 0:512],
                                     mask=[i ^ 1 for i in range(32)])
                    e.tensor_tensor(out=stg[:, 0:512], in0=stg[:, 0:512],
                                    in1=ctab[:, cs], op=ALU.mult)
                    e.tensor_tensor(out=stg[:, 512:1024], in0=stg[:, 512:1024],
                                    in1=stab[:, cs], op=ALU.mult)
                    e.tensor_tensor(out=stg[:, 0:512], in0=stg[:, 0:512],
                                    in1=stg[:, 512:1024], op=ALU.add)
                    e.tensor_copy(out=xqstg[:, 0:512], in_=stg[:, 0:512])
                    e.tensor_tensor(out=xqstg[:, 512:1024],
                                    in0=stg[:, 0:512], in1=xqstg[:, 0:512],
                                    op=ALU.subtract).then_inc(vR3.h, 1)
                on("vector", p3_v)
                vR3.inc()
                nvr = vR3.n

                def p3_g(e, nvr=nvr, mc=mc, nt=nt):
                    e.wait_ge(vR3.h, nvr)
                    dd = qd if mc < 8 else kd
                    e.dma_start(out=dd[0, mc % 8, :, nt * 512:(nt + 1) * 512],
                                in_=xqstg[:, 0:512]).then_inc(dSp.h, 16)
                    e.dma_start(out=dd[1, mc % 8, :, nt * 512:(nt + 1) * 512],
                                in_=xqstg[:, 512:1024]).then_inc(dSp.h, 16)
                on("gpsimd", p3_g)
                dSp.inc(32)

        for half in range(2):
            def p3b_sy(e, half=half, w_cons=tV.n, w_wo=tWo.n, w_tb=tTB.n):
                if w_cons:
                    e.wait_ge(tV.h, w_cons)
                if w_wo:
                    e.wait_ge(tWo.h, w_wo)
                e.wait_ge(tTB.h, w_tb)
                e.dma_start(out=WBv[:], in_=wvt[half]).then_inc(dW.h, 16)
            on("sync", p3b_sy)
            dW.inc(16)
            n_dwv = dW.n

            for tc in range(TC):
                idx2 = half * TC + tc
                b = idx2 % 2
                w_ev = sV3.n - 1

                def p3b_t(e, tc=tc, b=b, n_dwv=n_dwv, w_ev=w_ev, w_xn=sP1c.n):
                    e.wait_ge(dW.h, n_dwv)
                    e.wait_ge(sP1c.h, w_xn)
                    if w_ev >= 1:
                        e.wait_ge(sV3.h, w_ev)
                    xv = XN.rearrange("p (k t) -> p k t", k=KC)
                    wvv = WBv.rearrange("p (k c) -> p k c", k=KC)
                    last = None
                    for k in range(KC):
                        last = e.matmul(ps[b][:],
                                        xv[:, k, tc * 128:tc * 128 + 128],
                                        wvv[:, k, :],
                                        start=(k == 0), stop=(k == KC - 1))
                    last.then_inc(tV.h, 1)
                on("tensor", p3b_t)
                tV.inc()
                ntv = tV.n
                w_f = dSp.n

                def p3b_s(e, ntv=ntv, b=b, w_f=w_f):
                    e.wait_ge(tV.h, ntv)
                    if w_f:
                        e.wait_ge(dSp.h, w_f)
                    e.copy(out=fstg[:, b * 512:(b + 1) * 512],
                           in_=ps[b][:]).then_inc(sV3.h, 1)
                on("scalar", p3b_s)
                sV3.inc()
                nsv = sV3.n

                def p3b_g(e, nsv=nsv, tc=tc, half=half, b=b):
                    e.wait_ge(sV3.h, nsv)
                    e.dma_start(out=vd[:, tc, half * 512:(half + 1) * 512],
                                in_=fstg[:, b * 512:(b + 1) * 512]
                                ).then_inc(dSp.h, 16)
                on("gpsimd", p3b_g)
                dSp.inc(16)

        allcells = []
        for h in range(NH):
            for qt in range(NT):
                nkc = 4 * qt + 4
                for kc in range(nkc):
                    diag = kc >= 4 * qt
                    j = (kc - 4 * qt) * 128 if diag else 0
                    allcells.append((h, qt, kc, diag, j, kc == 0,
                                     kc == nkc - 1,
                                     qt == NT - 1 and kc == nkc - 1))

        def hbuf(h):
            if h % 2 == 0:
                return (qha, qlb, khl, vh)
            return (qha2, qlb2, khl2, vh2)

        def emit_batch(h, waits):
            def p4_sy(e, h=h, waits=waits):
                for sh, cnt in waits:
                    if cnt:
                        e.wait_ge(sh, cnt)
                qa, ql, kb, vb = hbuf(h)
                r0 = (h % 2) * 64
                src_q = qd[:, h // 2, r0:r0 + 64, :]
                src_k = kd[:, h // 2, r0:r0 + 64, :]
                e.dma_start(out=qa[0:64, :], in_=src_q[0]).then_inc(dH4.h, 16)
                e.dma_start(out=qa[64:128, :], in_=src_q[0]).then_inc(dH4.h, 16)
                e.dma_start(out=ql[0:64, :], in_=src_q[1]).then_inc(dH4.h, 16)
                e.dma_start(out=kb[0:64, :], in_=src_k[0]).then_inc(dH4.h, 16)
                e.dma_start(out=kb[64:128, :], in_=src_k[1]).then_inc(dH4.h, 16)
                vv = vb.rearrange("p (t c) -> p t c", t=TC)
                e.dma_start(out=vv[:, 0:8, :],
                            in_=vd[:, 0:8, h * 64:h * 64 + 64]
                            ).then_inc(dH4.h, 16)
                e.dma_start(out=vv[:, 8:16, :],
                            in_=vd[:, 8:16, h * 64:h * 64 + 64]
                            ).then_inc(dH4.h, 16)
            on("sync", p4_sy)
            dH4.inc(112)
            return dH4.n

        w_sp = dSp.n
        w_b01 = [(dSp.h, w_sp), (tFf.h, tFf.n), (tTr.h, tTr.n),
                 (vO.h, vO.n), (dOut.h, dOut.n)]

        def p4_vz(e, w_ff=tFf.n):
            if w_ff:
                e.wait_ge(tFf.h, w_ff)
            e.memset(qlb2[64:128, :], 0.0)
            e.tensor_tensor(out=dumw[:], in0=dumw[:], in1=dumw[:],
                            op=ALU.max).then_inc(dH4.h, 16)
        on("vector", p4_vz)
        dH4.inc(16)
        ndh = {0: emit_batch(0, w_b01), 1: emit_batch(1, w_b01)}

        def emit_ta(a):
            (ci_, h_, qt_, kc_, diag_, j_, first_, last_, eslot_, nvm_,
             w_se_) = a
            w_pz = sZ.n

            def p4_ta(e, h=h_, kc=kc_, j=j_, eslot=eslot_, w_se=w_se_,
                      nvm=nvm_, diag=diag_, first=first_, last_kc=last_,
                      w_pz=w_pz):
                e.wait_ge(sExp.h, w_se)
                if diag:
                    e.wait_ge(vMk.h, nvm)
                if first and w_pz:
                    e.wait_ge(sZ.h, w_pz)
                vb = hbuf(h)[3]
                esl = ebsl(eslot, j)
                e.matmul(psY[:, j:512], vb[:, kc * 64:(kc + 1) * 64],
                         esl, start=first, stop=last_kc)
                e.matmul(psZ[:, j:512], ones64[:, 0:64], esl,
                         start=first, stop=last_kc).then_inc(tZ.h, 1)
            on("tensor", p4_ta)
            tZ.inc()

        def emit_qt_tail(qt_, h_):
            ntz = tZ.n
            w_yst = dSp.n
            w_zr = vY.n

            def p4_sz(e, ntz=ntz, w_zr=w_zr):
                e.wait_ge(tZ.h, ntz)
                if w_zr:
                    e.wait_ge(vY.h, w_zr)
                e.copy(out=zsb[:], in_=psZ[:])
                e.copy(out=ystr[:], in_=psY[:]).then_inc(sZ.h, 1)
            on("scalar", p4_sz)
            sZ.inc()
            nsz = sZ.n

            def p4_vy(e, nsz=nsz, w_yst=w_yst):
                e.wait_ge(sZ.h, nsz)
                if w_yst:
                    e.wait_ge(dSp.h, w_yst)
                e.reciprocal(out=zsb[:], in_=zsb[:])
                e.tensor_tensor(out=yst[:], in0=ystr[:], in1=zsb[:],
                                op=ALU.mult).then_inc(vY.h, 1)
            on("vector", p4_vy)
            vY.inc()
            nvy = vY.n

            def p4_gy(e, nvy=nvy, h=h_, qt=qt_):
                e.wait_ge(vY.h, nvy)
                e.dma_start(out=yd[(h % 2) * 64:(h % 2) * 64 + 64, h // 2,
                                   qt * 512:(qt + 1) * 512],
                            in_=yst[:]).then_inc(dSp.h, 16)
            on("gpsimd", p4_gy)
            dSp.inc(16)

        LAG = 3
        pend = []
        expcnt = {}
        for ci, (h, qt, kc, diag, j, first, last, hlast) in \
                enumerate(allcells):
            seq_last = (ci == len(allcells) - 1)
            sb = 4 + ci % 2
            eslot = ci % 4
            w_exp = sExp.n - 1

            def p4_ts(e, h=h, qt=qt, kc=kc, j=j, sb=sb, n_dh=ndh[h],
                      w_exp=w_exp):
                e.wait_ge(dH4.h, n_dh)
                if w_exp >= 1:
                    e.wait_ge(sExp.h, w_exp)
                qa, ql, kb, _ = hbuf(h)
                ksl = kb[:, kc * 128:(kc + 1) * 128]
                qsl = slice(qt * 512 + j, (qt + 1) * 512)
                e.matmul(ps[sb][:, j:512], ksl, qa[:, qsl],
                         start=True, stop=False)
                e.matmul(ps[sb][:, j:512], ksl, ql[:, qsl],
                         start=False, stop=True).then_inc(tS.h, 1)
            on("tensor", p4_ts)
            tS.inc()
            nts = tS.n
            w_ec = tZ.n

            def p4_se(e, nts=nts, sb=sb, j=j, eslot=eslot, w_ec=w_ec,
                      seq_last=seq_last):
                e.wait_ge(tS.h, nts)
                if w_ec >= 1:
                    e.wait_ge(tZ.h, w_ec)
                act = e.activation(
                    out=ebsl(eslot, j),
                    in_=ps[sb][:, j:512], func=AF.Exp, scale=0.125,
                    bias=cst[:, 3:4])
                if seq_last:
                    sguard(e, sExp.h)
                else:
                    act.then_inc(sExp.h, 1)
            on("scalar", p4_se)
            sExp.inc()
            expcnt[ci] = sExp.n
            nse = sExp.n

            if diag:
                def p4_vm(e, nse=nse, eslot=eslot, j=j):
                    e.wait_ge(sExp.h, nse)
                    sl = ebsl(eslot, j, j + 128)
                    e.tensor_tensor(out=sl, in0=sl, in1=trimask[:],
                                    op=ALU.mult)
                    vguard(e, vMk.h)
                on("vector", p4_vm)
                vMk.inc()
            nvm = vMk.n

            pend.append((ci, h, qt, kc, diag, j, first, last, eslot, nvm,
                         None))
            if len(pend) > LAG:
                a = pend.pop(0)
                emit_ta(a[:10] + (expcnt[a[0] + 1],))
                if a[7]:
                    emit_qt_tail(a[2], a[1])
                    if allcells[a[0]][7]:
                        hh = a[1]
                        if hh + 2 < NH:
                            ndh[hh + 2] = emit_batch(
                                hh + 2, [(tZ.h, tZ.n)])

        while pend:
            a = pend.pop(0)
            w_se = expcnt.get(a[0] + 1, sExp.n)
            emit_ta(a[:10] + (w_se,))
            if a[7]:
                emit_qt_tail(a[2], a[1])

        wo_mm = {}
        for qt in range(NT):
            def p6_sy(e, qt=qt, w_y=dSp.n, w_wb=tWo.n, w_tv=tV.n):
                e.wait_ge(dSp.h, w_y)
                e.wait_ge(tV.h, w_tv)
                if w_wb:
                    e.wait_ge(tWo.h, w_wb)
                wbv = WBv.rearrange("p (k c) -> p k c", k=KC)
                for k in range(KC):
                    e.dma_start(out=wbv[:, k, :],
                                in_=yd[:, k, qt * 512:(qt + 1) * 512]
                                ).then_inc(dY6.h, 16)
            on("sync", p6_sy)
            dY6.inc(16 * KC)
            n_dy = dY6.n

            for mc in range(KC):
                idx = qt * KC + mc
                wsel = wdb if idx % 2 == 0 else eb

                def p6_syw(e, mc=mc, wsel=wsel,
                           w_free=wo_mm.get(idx - 2, 0), w_qk=tQK.n,
                           w_ta=(tZ.n if idx < 2 else 0)):
                    e.wait_ge(tQK.h, w_qk)
                    if w_ta:
                        e.wait_ge(tZ.h, w_ta)
                    if w_free:
                        e.wait_ge(tWo.h, w_free)
                    e.dma_start(out=wsel[:], in_=wot[mc]).then_inc(dW.h, 16)
                on("sync", p6_syw)
                dW.inc(16)
                n_dwo = dW.n
                b = idx % 2
                w_ev = sWo.n - 1

                def p6_t(e, b=b, wsel=wsel, n_dwo=n_dwo, n_dy=n_dy,
                         w_ev=w_ev):
                    e.wait_ge(dW.h, n_dwo)
                    e.wait_ge(dY6.h, n_dy)
                    if w_ev >= 1:
                        e.wait_ge(sWo.h, w_ev)
                    wv_ = wsel.rearrange("p (k c) -> p k c", k=KC)
                    wbv = WBv.rearrange("p (k c) -> p k c", k=KC)
                    last = None
                    for k in range(KC):
                        last = e.matmul(ps[b][:], wv_[:, k, :], wbv[:, k, :],
                                        start=(k == 0), stop=(k == KC - 1))
                    last.then_inc(tWo.h, 1)
                on("tensor", p6_t)
                tWo.inc()
                wo_mm[idx] = tWo.n
                ntw = tWo.n
                w_f = dAF.n

                def p6_s(e, ntw=ntw, b=b, w_f=w_f, idx=idx, w_dsp=dSp.n):
                    e.wait_ge(tWo.h, ntw)
                    if idx < 2:
                        e.wait_ge(dSp.h, w_dsp)
                    if w_f:
                        e.wait_ge(dAF.h, w_f)
                    e.copy(out=fstg[:, b * 512:(b + 1) * 512],
                           in_=ps[b][:]).then_inc(sWo.h, 1)
                on("scalar", p6_s)
                sWo.inc()
                nsw = sWo.n

                def p6_g(e, nsw=nsw, mc=mc, qt=qt, b=b):
                    e.wait_ge(sWo.h, nsw)
                    e.dma_start(out=ad[:, mc, qt * 512:(qt + 1) * 512],
                                in_=fstg[:, b * 512:(b + 1) * 512]
                                ).then_inc(dAF.h, 16)
                on("gpsimd", p6_g)
                dAF.inc(16)

        def p7_sy(e, w_v=vTB.n, w_hl=tH.n):
            if w_hl:
                e.wait_ge(tH.h, w_hl)
            e.wait_ge(vTB.h, w_v)
            yv = YB.rearrange("p (a k c) -> p a k c", a=2, k=KC)
            e.dma_start(out=yv[:, 0], in_=hmt[0]).then_inc(dHM.h, 16)
            e.dma_start(out=yv[:, 1], in_=hmt[1]).then_inc(dHM.h, 16)
        on("sync", p7_sy)
        dHM.inc(32)
        n_hm = dHM.n

        def emit_t2(tc7, nvl):
            w_ev2 = sXq.n

            def p7_t2(e, nvl=nvl, w_ev2=w_ev2):
                e.wait_ge(vL7.h, nvl)
                if w_ev2 >= 1:
                    e.wait_ge(sXq.h, w_ev2)
                last = None
                for j in range(KC):
                    dst = ps[j // 4][:, (j % 4) * 128:(j % 4) * 128 + 128]
                    last = e.matmul(dst, xqtm[:, j * 128:(j + 1) * 128],
                                    identb[:], start=True, stop=True)
                last.then_inc(tXq.h, 1)
            on("tensor", p7_t2)
            tXq.inc()
            ntx = tXq.n
            w_xs = dXQ.n

            def p7_s2(e, ntx=ntx, w_xs=w_xs):
                e.wait_ge(tXq.h, ntx)
                if w_xs:
                    e.wait_ge(dXQ.h, w_xs)
                e.copy(out=xqstg[:, 0:512], in_=ps[0][:])
                e.copy(out=xqstg[:, 512:1024], in_=ps[1][:]).then_inc(sXq.h, 1)
            on("scalar", p7_s2)
            sXq.inc()
            nsx = sXq.n

            def p7_g(e, nsx=nsx, tc=tc7):
                e.wait_ge(sXq.h, nsx)
                xa = xqstg[:, 0:512].rearrange("p (k c) -> p k c", k=4)
                xb_ = xqstg[:, 512:1024].rearrange("p (k c) -> p k c", k=4)
                e.dma_start(out=xqd[:, 0:4, tc * 128:tc * 128 + 128],
                            in_=xa).then_inc(dXQ.h, 16)
                e.dma_start(out=xqd[:, 4:8, tc * 128:tc * 128 + 128],
                            in_=xb_).then_inc(dXQ.h, 16)
            on("gpsimd", p7_g)
            dXQ.inc(32)

        pend7 = None
        for tc in range(TC):
            pa = 2 + (tc % 2) * 2
            w_bank = sQ7.n - 1
            sx = 4 + (tc % 2) * 6

            def p7_t(e, tc=tc, pa=pa, n_hm=n_hm, w_bank=w_bank, w_xn=sP1c.n):
                e.wait_ge(dHM.h, n_hm)
                e.wait_ge(sP1c.h, w_xn)
                if w_bank >= 1:
                    e.wait_ge(sQ7.h, w_bank)
                xv = XN.rearrange("p (k t) -> p k t", k=KC)
                yv = YB.rearrange("p (a k c) -> p a k c", a=2, k=KC)
                last = None
                for half in range(2):
                    for k in range(KC):
                        last = e.matmul(ps[pa + half][:],
                                        xv[:, k, tc * 128:tc * 128 + 128],
                                        yv[:, half, k, :],
                                        start=(k == 0), stop=(k == KC - 1))
                last.then_inc(tH.h, 1)
            on("tensor", p7_t)
            tH.inc()
            nth = tH.n

            if pend7 is not None:
                emit_t2(*pend7)
                pend7 = None

            def p7_v1(e, nth=nth, pa=pa, sx=sx, tc=tc):
                e.wait_ge(tH.h, nth)
                e.tensor_reduce(out=stats[:, 4 * sx:4 * sx + 1], in_=ps[pa][:],
                                axis=AX.X, op=ALU.max,
                                apply_absolute_value=True)
                i0 = e.tensor_reduce(out=stats[:, 4 * sx + 4:4 * sx + 5],
                                     in_=ps[pa + 1][:], axis=AX.X, op=ALU.max,
                                     apply_absolute_value=True)
                vbar(e, i0)
                i1 = e.tensor_tensor(out=stats[:, 4 * sx:4 * sx + 1],
                                     in0=stats[:, 4 * sx:4 * sx + 1],
                                     in1=stats[:, 4 * sx + 4:4 * sx + 5],
                                     op=ALU.max)
                vbar(e, i1)
                i2 = e.tensor_tensor(out=stats[:, 4 * sx:4 * sx + 1],
                                     in0=stats[:, 4 * sx:4 * sx + 1],
                                     in1=cst[:, 6:7], op=ALU.max)
                vbar(e, i2)
                i3 = e.reciprocal(out=stats[:, 4 * sx + 8:4 * sx + 9],
                                  in_=stats[:, 4 * sx:4 * sx + 1])
                vbar(e, i3)
                i4 = e.tensor_tensor(out=stats[:, 4 * sx + 8:4 * sx + 9],
                                     in0=stats[:, 4 * sx + 8:4 * sx + 9],
                                     in1=cst[:, 10:11], op=ALU.mult)
                vbar(e, i4)
                e.reciprocal(out=sc_sb[:, tc:tc + 1],
                             in_=stats[:, 4 * sx + 8:4 * sx + 9])
                vguard(e, vQ7.h)
            on("vector", p7_v1)
            vQ7.inc()
            nvq = vQ7.n
            w_lad = vL7.n

            def p7_s1(e, nvq=nvq, pa=pa, sx=sx, w_lad=w_lad):
                e.wait_ge(vQ7.h, nvq)
                if w_lad:
                    e.wait_ge(vL7.h, w_lad)
                e.activation(out=scr1[:, 0:512], in_=ps[pa][:], func=AF.Identity,
                             scale=stats[:, 4 * sx + 8:4 * sx + 9], bias=cst[:, 4:5])
                e.activation(out=scr1[:, 512:1024], in_=ps[pa + 1][:],
                             func=AF.Identity, scale=stats[:, 4 * sx + 8:4 * sx + 9],
                             bias=cst[:, 4:5]).then_inc(sQ7.h, 1)
            on("scalar", p7_s1)
            sQ7.inc()
            nsq = sQ7.n
            w_xq = tXq.n

            def p7_v2(e, nsq=nsq, w_xq=w_xq, w_do=dOut.n):
                e.wait_ge(sQ7.h, nsq)
                if w_xq:
                    e.wait_ge(tXq.h, w_xq)
                if w_do:
                    e.wait_ge(dOut.h, w_do)
                e.tensor_copy(out=scr2[:], in_=scr1[:])
                for lv in (8.0, 4.0, 2.0, 1.0):
                    e.tensor_scalar(out=obuf[:], in0=scr2[:], scalar1=lv,
                                    scalar2=-lv, op0=ALU.is_ge, op1=ALU.mult)
                    e.tensor_tensor(out=scr2[:], in0=scr2[:], in1=obuf[:],
                                    op=ALU.add)
                e.scalar_tensor_tensor(out=xqtm[:], in0=scr2[:], scalar=8.0,
                                       in1=scr1[:], op0=ALU.add,
                                       op1=ALU.subtract).then_inc(vL7.h, 1)
            on("vector", p7_v2)
            vL7.inc()
            pend7 = (tc, vL7.n)
        emit_t2(*pend7)
        pend7 = None

        for qt in range(NT):
            def p7b_syx(e, qt=qt, w_xq=dXQ.n, w_cons=tFf.n):
                e.wait_ge(dXQ.h, w_xq)
                if w_cons:
                    e.wait_ge(tFf.h, w_cons)
                e.dma_start(out=xqb.rearrange("p (k c) -> p k c", k=KC),
                            in_=xqd[:, 0:KC, qt * 512:(qt + 1) * 512]
                            ).then_inc(dXB.h, 16)
            on("sync", p7b_syx)
            dXB.inc(16)
            n_xb = dXB.n

            for mc in range(KC):
                idx = qt * KC + mc
                slot = idx % 2

                def p7b_syn(e, mc=mc, slot=slot, w_tn=dTN.n,
                            w_cons=tFf.n - 1,
                            w_ts=(tS.n if idx < 2 else 0)):
                    e.wait_ge(dTN.h, w_tn)
                    if w_ts:
                        e.wait_ge(tS.h, w_ts)
                    if w_cons >= 1:
                        e.wait_ge(tFf.h, w_cons)
                    dst = ntb[:, slot * KC * 128:(slot + 1) * KC * 128]
                    e.dma_start(out=dst.rearrange("p (k c) -> p k c", k=KC),
                                in_=ternd[:, 0:KC, mc * 128:mc * 128 + 128]
                                ).then_inc(dNT.h, 16)
                on("sync", p7b_syn)
                dNT.inc(16)
                n_nt = dNT.n

                def p7b_rly(e, n_nt=n_nt, n_xb=n_xb):
                    e.wait_ge(dNT.h, n_nt)
                    e.wait_ge(dXB.h, n_xb)
                    e.tensor_tensor(out=dumw[:], in0=dumw[:], in1=dumw[:],
                                    op=ALU.max).then_inc(dRL.h, 1)
                on("vector", p7b_rly)
                dRL.inc()
                n_rl = dRL.n
                b = idx % 2
                w_ev = sFf.n - 1

                def p7b_t(e, b=b, slot=slot, n_rl=n_rl,
                          w_ev=w_ev, idx=idx, w_ff0=(sXq.n, sWo.n)):
                    e.wait_ge(dRL.h, n_rl)
                    if w_ev >= 1:
                        e.wait_ge(sFf.h, w_ev)
                    if idx % (NT * KC) < 2:
                        e.wait_ge(sXq.h, w_ff0[0])
                    ntv = ntb[:, slot * KC * 128:(slot + 1) * KC * 128
                              ].rearrange("p (k c) -> p k c", k=KC)
                    xqv = xqb.rearrange("p (k c) -> p k c", k=KC)
                    last = None
                    for k in range(KC):
                        last = e.matmul(ps[b][:], ntv[:, k, :], xqv[:, k, :],
                                        start=(k == 0), stop=(k == KC - 1))
                    last.then_inc(tFf.h, 1)
                on("tensor", p7b_t)
                tFf.inc()
                ntf = tFf.n
                w_f = dAF.n

                def p7b_s(e, ntf=ntf, b=b, mc=mc, w_f=w_f):
                    e.wait_ge(tFf.h, ntf)
                    if w_f:
                        e.wait_ge(dAF.h, w_f)
                    e.activation(out=fstg[:, b * 512:(b + 1) * 512],
                                 in_=ps[b][:], func=AF.Identity,
                                 scale=so_sb[:, mc:mc + 1]).then_inc(sFf.h, 1)
                on("scalar", p7b_s)
                sFf.inc()
                nsf = sFf.n

                def p7b_g(e, nsf=nsf, mc=mc, qt=qt, b=b):
                    e.wait_ge(sFf.h, nsf)
                    e.dma_start(out=fd[:, mc, qt * 512:(qt + 1) * 512],
                                in_=fstg[:, b * 512:(b + 1) * 512]
                                ).then_inc(dAF.h, 16)
                on("gpsimd", p7b_g)
                dAF.inc(16)

        p8_tt = {}
        p8_vv = {}
        p8_gg = {}
        for tc in range(TC):
            xd_ = xdb if tc % 2 == 0 else vh
            s1_ = scr1 if tc % 2 == 0 else XN[:, 0:1024]
            s2_ = scr2 if tc % 2 == 0 else XN[:, 1024:2048]
            stg_ = stg if tc % 2 == 0 else XN[:, 2048:3072]
            fa_ = fstg[:, 0:512] if tc % 2 == 0 else XN[:, 3072:3584]
            fb_ = fstg[:, 512:1024] if tc % 2 == 0 else XN[:, 3584:4096]

            def p8_sy(e, tc=tc, s=s, xd_=xd_, s1_=s1_, s2_=s2_,
                      w_cons=p8_vv.get(tc - 2, 0), w_all=dAF.n,
                      w_tr=(p8_tt.get(tc - 2, 0) if tc >= 2 else tTr.n),
                      w_do=(p8_gg.get(tc - 2, 0) if tc >= 2 else dOut.n),
                      row0=row0, w_sp1=sP1b.n):
                e.wait_ge(dAF.h, w_all)
                e.wait_ge(sP1b.h, w_sp1)
                if w_cons >= 1:
                    e.wait_ge(vO.h, w_cons)
                e.dma_start(out=xd_[:],
                            in_=xin[row0 + tc * 128: row0 + tc * 128 + 128, :]
                            ).then_inc(dX.h, 16)
                if w_tr:
                    e.wait_ge(tTr.h, w_tr)
                if w_do:
                    e.wait_ge(dOut.h, w_do)
                e.dma_start(out=s1_[:],
                            in_=ad[:, 0:KC, tc * 128:tc * 128 + 128]
                            ).then_inc(dX.h, 16)
                e.dma_start(out=s2_[:],
                            in_=fd[:, 0:KC, tc * 128:tc * 128 + 128]
                            ).then_inc(dX.h, 16)
            on("sync", p8_sy)
            dX.inc(48)
            n_d8 = dX.n
            w_ps = sTr.n

            def p8_rly(e, n_d8=n_d8):
                e.wait_ge(dX.h, n_d8)
                e.tensor_tensor(out=dumw[:], in0=dumw[:], in1=dumw[:],
                                op=ALU.max).then_inc(dRL.h, 1)
            on("vector", p8_rly)
            dRL.inc()
            n_rl8 = dRL.n

            def p8_t(e, n_rl8=n_rl8, s1_=s1_, s2_=s2_, w_ps=w_ps,
                     w_ff=sFf.n):
                e.wait_ge(dRL.h, n_rl8)
                e.wait_ge(sFf.h, w_ff)
                if w_ps:
                    e.wait_ge(sTr.h, w_ps)
                last = None
                for j in range(KC):
                    dst = ps[j // 4][:, (j % 4) * 128:(j % 4) * 128 + 128]
                    last = e.matmul(dst, s1_[:, j * 128:(j + 1) * 128],
                                    identf[:], start=True, stop=True)
                for j in range(KC):
                    dst = ps[2 + j // 4][:, (j % 4) * 128:(j % 4) * 128 + 128]
                    last = e.matmul(dst, s2_[:, j * 128:(j + 1) * 128],
                                    identf[:], start=True, stop=True)
                last.then_inc(tTr.h, 1)
            on("tensor", p8_t)
            tTr.inc()
            ntt = tTr.n
            p8_tt[tc] = tTr.n

            def p8_s(e, ntt=ntt, tc=tc, stg_=stg_, fa_=fa_, fb_=fb_,
                     w_dsp=dSp.n, w_vp=p8_vv.get(tc - 2, 0)):
                e.wait_ge(tTr.h, ntt)
                if tc == 0:
                    e.wait_ge(dSp.h, w_dsp)
                if w_vp:
                    e.wait_ge(vO.h, w_vp)
                e.copy(out=stg_[:, 0:512], in_=ps[0][:])
                e.copy(out=stg_[:, 512:1024], in_=ps[1][:])
                e.activation(out=fa_, in_=ps[2][:],
                             func=AF.Identity, scale=sc_sb[:, tc:tc + 1])
                e.activation(out=fb_, in_=ps[3][:],
                             func=AF.Identity,
                             scale=sc_sb[:, tc:tc + 1]).then_inc(sTr.h, 1)
            on("scalar", p8_s)
            sTr.inc()
            nst = sTr.n

            def p8_v(e, nst=nst, tc=tc, xd_=xd_, s1_=s1_, stg_=stg_,
                     fa_=fa_, fb_=fb_):
                e.wait_ge(sTr.h, nst)
                e.tensor_tensor(out=s1_[:, 0:512], in0=stg_[:, 0:512],
                                in1=fa_, op=ALU.add)
                e.tensor_tensor(out=s1_[:, 512:1024], in0=stg_[:, 512:1024],
                                in1=fb_, op=ALU.add)
                e.tensor_tensor(out=s1_[:], in0=s1_[:], in1=xd_[:],
                                op=ALU.add).then_inc(vO.h, 1)
            on("vector", p8_v)
            vO.inc()
            nvo = vO.n
            p8_vv[tc] = vO.n

            def p8_g(e, nvo=nvo, tc=tc, s1_=s1_, row0=row0):
                e.wait_ge(vO.h, nvo)
                e.dma_start(out=out[row0 + tc * 128: row0 + tc * 128 + 128, :],
                            in_=s1_[:]).then_inc(dOut.h, 16)
            on("gpsimd", p8_g)
            dOut.inc(16)
            p8_gg[tc] = dOut.n

    def fin(e, w=dOut.n):
        e.wait_ge(dOut.h, w)
    on("gpsimd", fin)

    with nc.Block() as block:
        for eng in ("sync", "gpsimd", "tensor", "vector", "scalar"):
            fns = ops[eng]

            def make(fns=fns):
                def section(e):
                    for fn in fns:
                        fn(e)
                return section
            getattr(block, eng)(make())

    return nc


def _hadamard(size):
    h = np.array([[1.0]], dtype=np.float64)
    while h.shape[0] < size:
        h = np.block([[h, h], [h, -h]])
    return (h / np.sqrt(size)).astype(np.float32)


def _host_inputs(L0_attn_W_qkv, L0_attn_W_o, L0_ffn_weight):
    Wqkv = np.ascontiguousarray(np.asarray(L0_attn_W_qkv, dtype=np.float32))
    Wo = np.ascontiguousarray(np.asarray(L0_attn_W_o, dtype=np.float32))
    Wf = np.ascontiguousarray(np.asarray(L0_ffn_weight, dtype=np.float32))

    perm64 = np.empty(64, np.int64)
    perm64[0::2] = np.arange(32)
    perm64[1::2] = np.arange(32, 64)
    colperm = np.arange(3072)
    for h in range(NH):
        colperm[64 * h:64 * h + 64] = 64 * h + perm64
        colperm[1024 + 64 * h:1024 + 64 * h + 64] = 1024 + 64 * h + perm64
    Wp = Wqkv[:, colperm]

    wqk = np.ascontiguousarray(
        Wp[:, :2048].reshape(KC, 128, 16, 128).transpose(2, 1, 0, 3))
    wvt = np.ascontiguousarray(
        Wp[:, 2048:].reshape(KC, 128, 2, 512).transpose(2, 1, 0, 3))
    wot = np.ascontiguousarray(
        Wo.reshape(KC, 128, KC, 128).transpose(2, 1, 0, 3))
    wft = np.ascontiguousarray(Wf.reshape(KC, 128, D))
    H = _hadamard(D)
    hmt = np.ascontiguousarray(
        H.reshape(KC, 128, 2, 512).transpose(2, 1, 0, 3))
    iota = np.ascontiguousarray(
        np.broadcast_to(np.arange(T, dtype=np.float32), (128, T)))
    f = (np.arange(128) % 64) // 2
    theta = 1.0 / (10000.0 ** (f.astype(np.float64) / 32.0))
    th = theta.astype(np.float32)
    tl = (theta - th.astype(np.float64)).astype(np.float32)
    csf = np.where(np.arange(128) % 2 == 0, np.float32(-1.0), np.float32(1.0))
    ones = np.ones(128, np.float32)
    cols = [th, tl, csf.astype(np.float32),
            -12.0 * ones, 8.5 * ones,
            np.float32(2.0 ** -10) * ones, np.float32(1e-5) * ones,
            0.5 * ones, -1.0 * ones,
            np.float32(np.float32(1.0) / np.float32(7.0)) * ones,
            7.0 * ones, 0.0 * ones]
    thet = np.ascontiguousarray(np.stack(cols, axis=1).astype(np.float32))
    return wqk, wvt, wot, wft, hmt, iota, thet


def kernel(L0_attn_x, L0_attn_W_qkv, L0_attn_W_o, L0_ffn_weight):
    x = np.ascontiguousarray(np.asarray(L0_attn_x, dtype=np.float32))
    wqk, wvt, wot, wft, hmt, iota, thet = _host_inputs(
        L0_attn_W_qkv, L0_attn_W_o, L0_ffn_weight)

    nc = build_nc()
    in_maps = []
    for c in range(8):
        xs = np.ascontiguousarray(x[2 * c:2 * c + 2].reshape(NTOK, D))
        in_maps.append({"xin": xs, "wqk": wqk, "wvt": wvt, "wot": wot,
                        "wft": wft, "hmt": hmt, "iota": iota, "thet": thet})
    res = run_bass_kernel_spmd(nc, in_maps, core_ids=list(range(8)))
    outs = [np.asarray(r["out"]) for r in res.results]
    full = np.stack(outs, axis=0).reshape(16, T, D)
    return full


# version 15
# speedup vs baseline: 1.9071x; 1.0197x over previous
import math
import numpy as np
import ml_dtypes
import concourse.bass as bass
import concourse.mybir as mybir
from concourse.bass_utils import run_bass_kernel_spmd

F32 = mybir.dt.float32
BF16 = mybir.dt.bfloat16
AF = mybir.ActivationFunctionType
ALU = mybir.AluOpType
AX = mybir.AxisListType

T = 2048
D = 1024
NH = 16
DH = 64
NSEQ = 2
NTOK = NSEQ * T
KC = 8
TC = T // 128
NT = T // 512
B_EXP = 12.0


class Sem:
    def __init__(self, nc, name):
        self.h = nc.alloc_semaphore(name)
        self.n = 0

    def inc(self, step=1):
        self.n += step
        return self.n


def build_nc(debug=False):
    nc = bass.Bass("TRN2", target_bir_lowering=False,
                   detect_race_conditions=False)
    ikind = "ExternalOutput" if debug else "Internal"

    xin = nc.dram_tensor("xin", [NTOK, D], F32, kind="ExternalInput").ap()
    wqk = nc.dram_tensor("wqk", [16, 128, KC, 128], F32, kind="ExternalInput").ap()
    wvt = nc.dram_tensor("wvt", [2, 128, KC, 512], F32, kind="ExternalInput").ap()
    wot = nc.dram_tensor("wot", [KC, 128, KC, 128], F32, kind="ExternalInput").ap()
    wft = nc.dram_tensor("wft", [KC, 128, D], F32, kind="ExternalInput").ap()
    hmt = nc.dram_tensor("hmt", [2, 128, KC, 512], F32, kind="ExternalInput").ap()
    iota = nc.dram_tensor("iota", [128, T], F32, kind="ExternalInput").ap()
    thet = nc.dram_tensor("thet", [128, 12], F32, kind="ExternalInput").ap()
    out = nc.dram_tensor("out", [NTOK, D], F32, kind="ExternalOutput").ap()

    qd = nc.dram_tensor("qd", [2, KC, 128, T], BF16, kind=ikind).ap()
    kd = nc.dram_tensor("kd", [2, KC, 128, T], BF16, kind=ikind).ap()
    vd = nc.dram_tensor("vd", [128, TC, D], F32, kind=ikind).ap()
    yd = nc.dram_tensor("yd", [128, KC, T], F32, kind=ikind).ap()
    ad = nc.dram_tensor("ad", [128, KC, T], F32, kind=ikind).ap()
    fd = nc.dram_tensor("fd", [128, KC, T], F32, kind=ikind).ap()
    xqd = nc.dram_tensor("xqd", [128, KC, T], BF16, kind=ikind).ap()
    ternd = nc.dram_tensor("ternd", [128, KC, D], BF16, kind=ikind).ap()
    if debug:
        xnd_dbg = nc.dram_tensor("xnd_dbg", [128, KC * T], F32,
                                 kind="ExternalOutput").ap()
        st_dbg = nc.dram_tensor("st_dbg", [128, 64], F32,
                                kind="ExternalOutput").ap()
        sb_dbg = nc.dram_tensor("sb_dbg", [128, KC + TC], F32,
                                kind="ExternalOutput").ap()

    A = nc.alloc_sbuf_tensor
    XN = A("XN", [128, KC * T], F32).ap()
    YB = A("YB", [128, KC * D], F32).ap()
    WBv = A("WBv", [128, KC * 512], F32).ap()

    ctab = A("ctab", [128, T], F32).ap()
    stab = A("stab", [128, T], F32).ap()
    so_sb = A("so_sb", [128, KC], F32).ap()
    sc_sb = A("sc_sb", [128, TC], F32).ap()
    trimask = A("trimask", [128, 128], F32).ap()
    identf = A("identf", [128, 128], F32).ap()
    identb = A("identb", [128, 128], BF16).ap()
    ones64 = A("ones64", [128, 64], F32).ap()
    cst = A("cst", [128, 12], F32).ap()
    stats = A("stats", [128, 64], F32).ap()
    dum = A("dum", [128, 8], F32).ap()
    dumw = A("dumw", [128, 384], F32).ap()

    wdb = A("wdb", [128, KC * 128], F32).ap()
    xdb = A("xdb", [128, D], F32).ap()
    stg = A("stg", [128, 2 * 512], F32).ap()
    yst = stg[0:64, 0:512]
    qha = A("qha", [128, T], BF16).ap()
    qlb = A("qlb", [128, T], BF16).ap()
    khl = A("khl", [128, T], BF16).ap()
    vh = A("vh", [128, TC * DH], F32).ap()
    eb = A("eb", [128, 2 * 512], F32).ap()
    zsb = A("zsb", [64, 512], F32).ap()
    ystr = A("ystr", [64, 512], F32).ap()
    scr1 = A("scr1", [128, D], F32).ap()
    scr2 = A("scr2", [128, D], F32).ap()
    xqtm = A("xqtm", [128, D], BF16).ap()
    obuf = A("obuf", [128, D], F32).ap()
    fstg = A("fstg", [128, 2 * 512], F32).ap()
    xqb = A("xqb", [128, KC * 512], BF16).ap()
    ntb = A("ntb", [128, 2 * KC * 128], BF16).ap()
    xqstg = A("xqstg", [128, 2 * 512], BF16).ap()
    ntstg = A("ntstg", [128, KC * 128], BF16).ap()

    qha2 = xqb[:, 0:T]
    qlb2 = xqb[:, T:2 * T]
    khl2 = ntb[:, 0:T]
    vh2 = obuf

    def ebsl(eslot, j, end=512):
        if eslot < 2:
            base = eslot * 512
            return eb[:, base + j:base + end]
        base = (eslot - 2) * 512
        return fstg[:, base + j:base + end]

    P = nc.alloc_psum_tensor
    ps = [P(f"ps{i}", [128, 512], F32).ap() for i in range(6)]
    psY = P("psY", [64, 512], F32).ap()
    psZ = P("psZ", [64, 512], F32).ap()

    ops = {e: [] for e in ("sync", "gpsimd", "tensor", "vector", "scalar")}

    def on(eng, fn):
        ops[eng].append(fn)

    sems = {}

    def sem(name):
        if name not in sems:
            sems[name] = Sem(nc, name)
        return sems[name]

    tpi = 2.0 * math.pi
    c2ph = float(np.float32(tpi)); c2pl = float(np.float32(tpi - float(np.float32(tpi))))
    cph = float(np.float32(math.pi)); cpl = float(np.float32(math.pi - float(np.float32(math.pi))))
    i2p = 1.0 / tpi
    ci2h = float(np.float32(i2p)); ci2l = float(np.float32(i2p - float(np.float32(i2p))))
    p2 = math.pi / 2
    cp2h = float(np.float32(p2)); cp2l = float(np.float32(p2 - float(np.float32(p2))))
    CLAMP = 3.1415925
    C7 = float(np.float32(1.0 / 7.0))
    INV1024 = float(np.float32(2.0 ** -10))

    idf_np = np.eye(128, dtype=np.float32)
    idb_np = np.eye(128, dtype=ml_dtypes.bfloat16).view(np.uint16)
    tri_np = (np.arange(128)[:, None] <= np.arange(128)[None, :]).astype(np.float32)
    t_idf = nc.inline_tensor(idf_np, "idf")
    t_idb = nc.inline_tensor(idb_np, "idb")
    t_tri = nc.inline_tensor(tri_np, "tri")

    dTB = sem("dTB"); vTB = sem("vTB"); sTB = sem("sTB"); tTB = sem("tTB")
    dTN = sem("dTN")

    def tb_load(e):
        e.dma_start(out=ctab[:], in_=iota).then_inc(dTB.h, 16)
        e.dma_start(out=cst[:], in_=thet).then_inc(dTB.h, 16)
        e.dma_start(out=identf[:], in_=t_idf.ap()).then_inc(dTB.h, 16)
        e.dma_start(out=identb[:], in_=t_idb.ap().bitcast(BF16)).then_inc(dTB.h, 16)
        e.dma_start(out=trimask[:], in_=t_tri.ap()).then_inc(dTB.h, 16)
    on("sync", tb_load)
    dTB.inc(16 * 5)
    n_tbload = dTB.n

    ANG = YB[:, 0:T]
    WRK = YB[:, T:2 * T]
    WK2 = YB[:, 2 * T:3 * T]
    RR = YB[:, 3 * T:4 * T]

    for is_cos in (False, True):
        def tab_v1(e, is_cos=is_cos, w=n_tbload):
            e.wait_ge(dTB.h, w)
            e.tensor_scalar(out=ANG, in0=ctab[:], scalar1=cst[:, 0:1],
                            scalar2=None, op0=ALU.mult)
            e.tensor_scalar(out=WRK, in0=ctab[:], scalar1=cst[:, 1:2],
                            scalar2=None, op0=ALU.mult)
            e.tensor_tensor(out=ANG, in0=ANG, in1=WRK, op=ALU.add)
            if is_cos:
                e.tensor_scalar(out=ANG, in0=ANG, scalar1=cp2h, scalar2=cp2l,
                                op0=ALU.add, op1=ALU.add)
            e.tensor_scalar(out=WRK, in0=ANG, scalar1=ci2h, scalar2=None,
                            op0=ALU.mult)
            e.tensor_scalar(out=WK2, in0=ANG, scalar1=ci2l, scalar2=None,
                            op0=ALU.mult)
            e.tensor_tensor(out=WRK, in0=WRK, in1=WK2, op=ALU.add)
            e.tensor_scalar(out=WRK, in0=WRK, scalar1=0.5, scalar2=None,
                            op0=ALU.add)
            e.tensor_copy(out=WK2, in_=WRK)
            for lv in (256., 128., 64., 32., 16., 8., 4., 2., 1.):
                e.tensor_scalar(out=RR, in0=WRK, scalar1=lv, scalar2=-lv,
                                op0=ALU.is_ge, op1=ALU.mult)
                e.tensor_tensor(out=WRK, in0=WRK, in1=RR, op=ALU.add)
            e.tensor_tensor(out=WK2, in0=WK2, in1=WRK, op=ALU.subtract)
            e.scalar_tensor_tensor(out=RR, in0=WK2, scalar=-c2ph, in1=ANG,
                                   op0=ALU.mult, op1=ALU.add)
            e.scalar_tensor_tensor(out=WRK, in0=WK2, scalar=-c2pl, in1=RR,
                                   op0=ALU.mult, op1=ALU.add)
            e.tensor_scalar(out=WRK, in0=WRK, scalar1=CLAMP, scalar2=-CLAMP,
                            op0=ALU.min, op1=ALU.max).then_inc(vTB.h, 1)
        on("vector", tab_v1)
        vTB.inc()
        nv = vTB.n

        def tab_s(e, nv=nv):
            e.wait_ge(vTB.h, nv)
            e.activation(out=RR, in_=WRK, func=AF.Sin).then_inc(sTB.h, 1)
        on("scalar", tab_s)
        sTB.inc()
        ns = sTB.n

        def tab_v2(e, is_cos=is_cos, ns=ns):
            e.wait_ge(sTB.h, ns)
            if is_cos:
                e.tensor_tensor(out=ctab[:], in0=RR, in1=RR,
                                op=ALU.max).then_inc(vTB.h, 1)
            else:
                e.tensor_scalar(out=stab[:], in0=RR, scalar1=cst[:, 2:3],
                                scalar2=None, op0=ALU.mult).then_inc(vTB.h, 1)
        on("vector", tab_v2)
        vTB.inc()

    def tb_consts(e):
        e.memset(ones64[:], 1.0)
        e.memset(dumw[:], 0.0)
        e.memset(stats[:], 0.0)
        e.memset(dum[:], 0.0)
        e.memset(sc_sb[:], 0.0)
        e.memset(so_sb[:], 0.0)
        e.memset(qlb[64:128, :], 0.0)
    on("vector", tb_consts)

    for oc in range(KC):
        w_t = tTB.n

        def f_sy(e, oc=oc, w_t=w_t):
            if oc > 0:
                e.wait_ge(tTB.h, w_t)
            e.dma_start(out=WBv[:, 0:D], in_=wft[oc]).then_inc(dTB.h, 16)
        on("sync", f_sy)
        dTB.inc(16)
        n_wf = dTB.n

        def f_v(e, oc=oc, n_wf=n_wf):
            e.wait_ge(dTB.h, n_wf)
            wt_ = WBv[:, 0:D]
            i0 = e.tensor_reduce(out=stats[:, 16:17], in_=wt_, axis=AX.X,
                                 op=ALU.add, apply_absolute_value=True)
            vbar(e, i0)
            i1 = e.tensor_tensor(out=stats[:, 16:17], in0=stats[:, 16:17],
                                 in1=cst[:, 5:6], op=ALU.mult)
            vbar(e, i1)
            i2 = e.tensor_tensor(out=stats[:, 16:17], in0=stats[:, 16:17],
                                 in1=cst[:, 6:7], op=ALU.max)
            vbar(e, i2)
            e.tensor_tensor(out=so_sb[:, oc:oc + 1], in0=stats[:, 16:17],
                            in1=stats[:, 16:17], op=ALU.max)
            i3 = e.tensor_tensor(out=stats[:, 20:21], in0=stats[:, 16:17],
                                 in1=cst[:, 7:8], op=ALU.mult)
            vbar(e, i3)
            i4 = e.tensor_tensor(out=stats[:, 24:25], in0=stats[:, 20:21],
                                 in1=cst[:, 8:9], op=ALU.mult)
            vbar(e, i4)
            e.tensor_scalar(out=scr1[:], in0=wt_, scalar1=stats[:, 20:21],
                            scalar2=None, op0=ALU.is_gt)
            e.tensor_scalar(out=scr2[:], in0=wt_, scalar1=stats[:, 24:25],
                            scalar2=None, op0=ALU.is_lt)
            e.tensor_tensor(out=xqtm[:], in0=scr2[:], in1=scr1[:],
                            op=ALU.subtract).then_inc(vTB.h, 1)
        on("vector", f_v)
        vTB.inc()
        n_vt = vTB.n
        w_s = sTB.n

        def f_t(e, n_vt=n_vt, w_s=w_s, oc=oc):
            e.wait_ge(vTB.h, n_vt)
            if oc > 0:
                e.wait_ge(sTB.h, w_s)
            last = None
            for j in range(KC):
                dst = ps[j // 4][:, (j % 4) * 128:(j % 4) * 128 + 128]
                last = e.matmul(dst, xqtm[:, j * 128:(j + 1) * 128], identb[:],
                                start=True, stop=True)
            last.then_inc(tTB.h, 1)
        on("tensor", f_t)
        tTB.inc()
        n_tt = tTB.n
        w_dn = dTN.n

        def f_s(e, oc=oc, n_tt=n_tt, w_dn=w_dn):
            e.wait_ge(tTB.h, n_tt)
            if w_dn:
                e.wait_ge(dTN.h, w_dn)
            last = None
            for j in range(KC):
                src = ps[j // 4][:, (j % 4) * 128:(j % 4) * 128 + 128]
                last = e.copy(out=ntstg[:, j * 128:(j + 1) * 128], in_=src)
            last.then_inc(sTB.h, 1)
        on("scalar", f_s)
        sTB.inc()
        n_fs = sTB.n

        def f_g(e, oc=oc, n_fs=n_fs):
            e.wait_ge(sTB.h, n_fs)
            nv_ = ntstg.rearrange("p (k c) -> p k c", k=KC)
            e.dma_start(out=ternd[:, 0:KC, oc * 128:oc * 128 + 128],
                        in_=nv_).then_inc(dTN.h, 16)
        on("gpsimd", f_g)
        dTN.inc(16)

    TBv, TBs, TBt, TBd = vTB.n, sTB.n, tTB.n, dTB.n

    dX = sem("dX"); dW = sem("dW"); dSp = sem("dSp"); dH4 = sem("dH4")
    dY6 = sem("dY6"); dAF = sem("dAF"); dOut = sem("dOut"); dHM = sem("dHM")
    dXQ = sem("dXQ"); dNT = sem("dNT"); dXB = sem("dXB")
    dRL = sem("dRL")
    tP1 = sem("tP1"); tQK = sem("tQK"); tV = sem("tV"); tS = sem("tS")
    tZ = sem("tZ"); tWo = sem("tWo"); tH = sem("tH"); tXq = sem("tXq")
    tTr = sem("tTr"); tFf = sem("tFf")
    sP1a = sem("sP1a"); sP1b = sem("sP1b"); sP1c = sem("sP1c")
    sP3 = sem("sP3"); sV3 = sem("sV3"); sExp = sem("sExp"); sZ = sem("sZ")
    sWo = sem("sWo"); sQ7 = sem("sQ7"); sXq = sem("sXq"); sTr = sem("sTr")
    sFf = sem("sFf")
    vLN1 = sem("vLN1"); vLN2 = sem("vLN2"); vR3 = sem("vR3"); vMk = sem("vMk")
    vY = sem("vY"); vQ7 = sem("vQ7"); vL7 = sem("vL7"); vO = sem("vO")

    vB = sem("vB")

    def vbar(e, inst):
        inst.then_inc(vB.h, 1)
        e.wait_ge(vB.h, vB.inc())

    def vguard(e, semh):
        e.tensor_tensor(out=dumw[:], in0=dumw[:], in1=dumw[:],
                        op=ALU.max).then_inc(semh, 1)

    sdmw = A("sdmw", [128, 448], F32).ap()

    def sguard(e, semh):
        e.copy(out=sdmw[:, 0:384], in_=dumw[:]).then_inc(semh, 1)

    for s in range(NSEQ):
        row0 = s * T

        for tc in range(TC):
            def p1_sy(e, tc=tc, s=s, w=sP1b.n, row0=row0, w_vo=vO.n):
                if s > 0 or tc > 0:
                    e.wait_ge(sP1b.h, w)
                if s > 0:
                    e.wait_ge(vO.h, w_vo)
                e.dma_start(out=xdb[:],
                            in_=xin[row0 + tc * 128: row0 + tc * 128 + 128, :]
                            ).then_inc(dX.h, 16)
            on("sync", p1_sy)
            dX.inc(16)
            n_dx = dX.n

            def p1_v1(e, n_dx=n_dx, w=TBv, w_sq7=sQ7.n):
                e.wait_ge(dX.h, n_dx)
                e.wait_ge(vTB.h, w)
                if w_sq7:
                    e.wait_ge(sQ7.h, w_sq7)
                i0 = e.tensor_reduce(out=stats[:, 16:17], in_=xdb[:],
                                     axis=AX.X, op=ALU.add)
                vbar(e, i0)
                e.tensor_tensor(out=stats[:, 20:21], in0=stats[:, 16:17],
                                in1=cst[:, 5:6],
                                op=ALU.mult).then_inc(vLN1.h, 1)
            on("vector", p1_v1)
            vLN1.inc()
            nv1 = vLN1.n

            def p1_s1(e, n_dx=n_dx, nv1=nv1, w=TBs, w_ttr=tTr.n):
                e.wait_ge(dX.h, n_dx)
                e.wait_ge(sTB.h, w)
                if w_ttr:
                    e.wait_ge(tTr.h, w_ttr)
                e.activation(out=scr2[:], in_=xdb[:], func=AF.Square,
                             accum_out=stats[:, 24:25])
                sguard(e, sP1a.h)
            on("scalar", p1_s1)
            sP1a.inc()
            ns1 = sP1a.n

            def p1_v2(e, nv1=nv1, ns1=ns1):
                e.wait_ge(sP1a.h, ns1)
                e.wait_ge(vLN1.h, nv1)
                e.tensor_tensor(out=stats[:, 28:29], in0=stats[:, 20:21],
                                in1=stats[:, 20:21], op=ALU.mult)
                i1 = e.tensor_tensor(out=stats[:, 32:33], in0=stats[:, 24:25],
                                     in1=cst[:, 5:6], op=ALU.mult)
                vbar(e, i1)
                i2 = e.tensor_tensor(out=stats[:, 32:33], in0=stats[:, 32:33],
                                     in1=stats[:, 28:29], op=ALU.subtract)
                vbar(e, i2)
                e.tensor_tensor(out=stats[:, 32:33], in0=stats[:, 32:33],
                                in1=cst[:, 6:7], op=ALU.add)
                vguard(e, vLN1.h)
            on("vector", p1_v2)
            vLN1.inc()
            nv2 = vLN1.n

            def p1_s2(e, nv2=nv2):
                e.wait_ge(vLN1.h, nv2)
                e.activation(out=stats[:, 36:37], in_=stats[:, 32:33],
                             func=AF.Sqrt)
                sguard(e, sP1a.h)
            on("scalar", p1_s2)
            sP1a.inc()
            ns2 = sP1a.n

            def p1_v3(e, ns2=ns2):
                e.wait_ge(sP1a.h, ns2)
                i0 = e.reciprocal(out=stats[:, 40:41], in_=stats[:, 36:37])
                vbar(e, i0)
                i1 = e.tensor_tensor(out=stats[:, 44:45], in0=stats[:, 32:33],
                                     in1=stats[:, 40:41], op=ALU.mult)
                vbar(e, i1)
                i2 = e.tensor_tensor(out=stats[:, 44:45], in0=stats[:, 36:37],
                                     in1=stats[:, 44:45], op=ALU.add)
                vbar(e, i2)
                i3 = e.tensor_tensor(out=stats[:, 44:45], in0=stats[:, 44:45],
                                     in1=cst[:, 7:8], op=ALU.mult)
                vbar(e, i3)
                i4 = e.reciprocal(out=stats[:, 48:49], in_=stats[:, 44:45])
                vbar(e, i4)
                i5 = e.tensor_tensor(out=stats[:, 52:53], in0=stats[:, 20:21],
                                     in1=stats[:, 48:49], op=ALU.mult)
                vbar(e, i5)
                e.tensor_tensor(out=stats[:, 52:53], in0=stats[:, 52:53],
                                in1=cst[:, 8:9], op=ALU.mult)
                vguard(e, vLN2.h)
            on("vector", p1_v3)
            vLN2.inc()
            nv3 = vLN2.n
            w_tp1 = tP1.n

            def p1_s3(e, nv3=nv3, w_tp1=w_tp1, w_ttr=tTr.n, w_do=dOut.n):
                e.wait_ge(vLN2.h, nv3)
                if w_tp1:
                    e.wait_ge(tP1.h, w_tp1)
                if w_ttr:
                    e.wait_ge(tTr.h, w_ttr)
                if w_do:
                    e.wait_ge(dOut.h, w_do)
                e.activation(out=scr1[:], in_=xdb[:], func=AF.Identity,
                             scale=stats[:, 48:49],
                             bias=stats[:, 52:53]).then_inc(sP1b.h, 1)
            on("scalar", p1_s3)
            sP1b.inc()
            ns3 = sP1b.n
            w_ev = sP1c.n

            def p1_t(e, ns3=ns3, w_ev=w_ev, w_tb=TBs):
                e.wait_ge(sP1b.h, ns3)
                e.wait_ge(sTB.h, w_tb)
                if w_ev:
                    e.wait_ge(sP1c.h, w_ev)
                last = None
                for j in range(KC):
                    dst = ps[j // 4][:, (j % 4) * 128:(j % 4) * 128 + 128]
                    last = e.matmul(dst, scr1[:, j * 128:(j + 1) * 128],
                                    identf[:], start=True, stop=True)
                last.then_inc(tP1.h, 1)
            on("tensor", p1_t)
            tP1.inc()
            nt1 = tP1.n

            def p1_s4(e, nt1=nt1, tc=tc):
                e.wait_ge(tP1.h, nt1)
                xv = XN.rearrange("p (k t) -> p k t", k=KC)
                e.copy(out=xv[:, 0:4, tc * 128:tc * 128 + 128],
                       in_=ps[0].rearrange("p (k t) -> p k t", k=4))
                e.copy(out=xv[:, 4:8, tc * 128:tc * 128 + 128],
                       in_=ps[1].rearrange("p (k t) -> p k t", k=4)
                       ).then_inc(sP1c.h, 1)
            on("scalar", p1_s4)
            sP1c.inc()

        if debug and s == 0:
            def dbg_dump(e, w_xn=sP1c.n):
                e.wait_ge(sP1c.h, w_xn)
                e.dma_start(out=xnd_dbg, in_=XN[:]).then_inc(dSp.h, 16)
                e.dma_start(out=st_dbg, in_=stats[:]).then_inc(dSp.h, 16)
                e.dma_start(out=sb_dbg[:, 0:KC], in_=so_sb[:]).then_inc(dSp.h, 16)
                e.dma_start(out=sb_dbg[:, KC:KC + TC], in_=sc_sb[:]).then_inc(dSp.h, 16)
            on("gpsimd", dbg_dump)
            dSp.inc(64)

        qk_mm = {}
        for mc in range(16):
            wsel = wdb if mc % 2 == 0 else eb

            def p3_syw(e, mc=mc, wsel=wsel, w_qk=qk_mm.get(mc - 2, 0),
                       w_wo=tWo.n):
                if w_qk:
                    e.wait_ge(tQK.h, w_qk)
                if w_wo:
                    e.wait_ge(tWo.h, w_wo)
                e.dma_start(out=wsel[:], in_=wqk[mc]).then_inc(dW.h, 16)
            on("sync", p3_syw)
            dW.inc(16)
            n_dw = dW.n

            for nt in range(NT):
                idx = mc * NT + nt
                b = 2 + idx % 2
                w_ps = sP3.n - 1
                w_xn = sP1c.n

                def p3_t(e, nt=nt, b=b, wsel=wsel, n_dw=n_dw, w_ps=w_ps,
                         w_xn=w_xn):
                    e.wait_ge(dW.h, n_dw)
                    e.wait_ge(sP1c.h, w_xn)
                    if w_ps >= 1:
                        e.wait_ge(sP3.h, w_ps)
                    xv = XN.rearrange("p (k t) -> p k t", k=KC)
                    wv_ = wsel.rearrange("p (k c) -> p k c", k=KC)
                    last = None
                    for k in range(KC):
                        last = e.matmul(ps[b][:], wv_[:, k, :],
                                        xv[:, k, nt * 512:(nt + 1) * 512],
                                        start=(k == 0), stop=(k == KC - 1))
                    last.then_inc(tQK.h, 1)
                on("tensor", p3_t)
                tQK.inc()
                ntq = tQK.n
                qk_mm[mc] = tQK.n
                w_stg = dSp.n

                def p3_s(e, ntq=ntq, b=b, w_stg=w_stg):
                    e.wait_ge(tQK.h, ntq)
                    if w_stg:
                        e.wait_ge(dSp.h, w_stg)
                    e.copy(out=stg[:, 0:512], in_=ps[b][:]).then_inc(sP3.h, 1)
                on("scalar", p3_s)
                sP3.inc()
                ns_ = sP3.n

                def p3_v(e, ns_=ns_, nt=nt):
                    e.wait_ge(sP3.h, ns_)
                    cs = slice(nt * 512, (nt + 1) * 512)
                    e.stream_shuffle(out=stg[:, 512:1024], in_=stg[:, 0:512],
                                     mask=[i ^ 1 for i in range(32)])
                    e.tensor_tensor(out=stg[:, 0:512], in0=stg[:, 0:512],
                                    in1=ctab[:, cs], op=ALU.mult)
                    e.tensor_tensor(out=stg[:, 512:1024], in0=stg[:, 512:1024],
                                    in1=stab[:, cs], op=ALU.mult)
                    e.tensor_tensor(out=stg[:, 0:512], in0=stg[:, 0:512],
                                    in1=stg[:, 512:1024], op=ALU.add)
                    e.tensor_copy(out=xqstg[:, 0:512], in_=stg[:, 0:512])
                    e.tensor_tensor(out=xqstg[:, 512:1024],
                                    in0=stg[:, 0:512], in1=xqstg[:, 0:512],
                                    op=ALU.subtract)
                    vguard(e, vR3.h)
                on("vector", p3_v)
                vR3.inc()
                nvr = vR3.n

                def p3_g(e, nvr=nvr, mc=mc, nt=nt):
                    e.wait_ge(vR3.h, nvr)
                    dd = qd if mc < 8 else kd
                    e.dma_start(out=dd[0, mc % 8, :, nt * 512:(nt + 1) * 512],
                                in_=xqstg[:, 0:512]).then_inc(dSp.h, 16)
                    e.dma_start(out=dd[1, mc % 8, :, nt * 512:(nt + 1) * 512],
                                in_=xqstg[:, 512:1024]).then_inc(dSp.h, 16)
                on("gpsimd", p3_g)
                dSp.inc(32)

        for half in range(2):
            def p3b_sy(e, half=half, w_cons=tV.n, w_wo=tWo.n, w_tb=tTB.n):
                if w_cons:
                    e.wait_ge(tV.h, w_cons)
                if w_wo:
                    e.wait_ge(tWo.h, w_wo)
                e.wait_ge(tTB.h, w_tb)
                e.dma_start(out=WBv[:], in_=wvt[half]).then_inc(dW.h, 16)
            on("sync", p3b_sy)
            dW.inc(16)
            n_dwv = dW.n

            for tc in range(TC):
                idx2 = half * TC + tc
                b = idx2 % 2
                w_ev = sV3.n - 1

                def p3b_t(e, tc=tc, b=b, n_dwv=n_dwv, w_ev=w_ev, w_xn=sP1c.n):
                    e.wait_ge(dW.h, n_dwv)
                    e.wait_ge(sP1c.h, w_xn)
                    if w_ev >= 1:
                        e.wait_ge(sV3.h, w_ev)
                    xv = XN.rearrange("p (k t) -> p k t", k=KC)
                    wvv = WBv.rearrange("p (k c) -> p k c", k=KC)
                    last = None
                    for k in range(KC):
                        last = e.matmul(ps[b][:],
                                        xv[:, k, tc * 128:tc * 128 + 128],
                                        wvv[:, k, :],
                                        start=(k == 0), stop=(k == KC - 1))
                    last.then_inc(tV.h, 1)
                on("tensor", p3b_t)
                tV.inc()
                ntv = tV.n
                w_f = dSp.n

                def p3b_s(e, ntv=ntv, b=b, w_f=w_f):
                    e.wait_ge(tV.h, ntv)
                    if w_f:
                        e.wait_ge(dSp.h, w_f)
                    e.copy(out=fstg[:, b * 512:(b + 1) * 512],
                           in_=ps[b][:]).then_inc(sV3.h, 1)
                on("scalar", p3b_s)
                sV3.inc()
                nsv = sV3.n

                def p3b_g(e, nsv=nsv, tc=tc, half=half, b=b):
                    e.wait_ge(sV3.h, nsv)
                    e.dma_start(out=vd[:, tc, half * 512:(half + 1) * 512],
                                in_=fstg[:, b * 512:(b + 1) * 512]
                                ).then_inc(dSp.h, 16)
                on("gpsimd", p3b_g)
                dSp.inc(16)

        allcells = []
        for h in range(NH):
            for qt in range(NT):
                nkc = 4 * qt + 4
                for kc in range(nkc):
                    diag = kc >= 4 * qt
                    j = (kc - 4 * qt) * 128 if diag else 0
                    allcells.append((h, qt, kc, diag, j, kc == 0,
                                     kc == nkc - 1,
                                     qt == NT - 1 and kc == nkc - 1))

        def hbuf(h):
            if h % 2 == 0:
                return (qha, qlb, khl, vh)
            return (qha2, qlb2, khl2, vh2)

        def emit_batch(h, waits):
            def p4_sy(e, h=h, waits=waits):
                for sh, cnt in waits:
                    if cnt:
                        e.wait_ge(sh, cnt)
                qa, ql, kb, vb = hbuf(h)
                r0 = (h % 2) * 64
                src_q = qd[:, h // 2, r0:r0 + 64, :]
                src_k = kd[:, h // 2, r0:r0 + 64, :]
                e.dma_start(out=qa[0:64, :], in_=src_q[0]).then_inc(dH4.h, 16)
                e.dma_start(out=qa[64:128, :], in_=src_q[0]).then_inc(dH4.h, 16)
                e.dma_start(out=ql[0:64, :], in_=src_q[1]).then_inc(dH4.h, 16)
                e.dma_start(out=kb[0:64, :], in_=src_k[0]).then_inc(dH4.h, 16)
                e.dma_start(out=kb[64:128, :], in_=src_k[1]).then_inc(dH4.h, 16)
                vv = vb.rearrange("p (t c) -> p t c", t=TC)
                e.dma_start(out=vv[:, 0:8, :],
                            in_=vd[:, 0:8, h * 64:h * 64 + 64]
                            ).then_inc(dH4.h, 16)
                e.dma_start(out=vv[:, 8:16, :],
                            in_=vd[:, 8:16, h * 64:h * 64 + 64]
                            ).then_inc(dH4.h, 16)
            on("sync", p4_sy)
            dH4.inc(112)
            return dH4.n

        w_sp = dSp.n
        w_b01 = [(dSp.h, w_sp), (tFf.h, tFf.n), (tTr.h, tTr.n),
                 (vO.h, vO.n), (dOut.h, dOut.n)]

        def p4_vz(e, w_ff=tFf.n):
            if w_ff:
                e.wait_ge(tFf.h, w_ff)
            e.memset(qlb2[64:128, :], 0.0)
            e.tensor_tensor(out=dumw[:], in0=dumw[:], in1=dumw[:],
                            op=ALU.max).then_inc(dH4.h, 16)
        on("vector", p4_vz)
        dH4.inc(16)
        ndh = {0: emit_batch(0, w_b01), 1: emit_batch(1, w_b01)}

        def emit_ta(a):
            (ci_, h_, qt_, kc_, diag_, j_, first_, last_, eslot_, nvm_,
             w_se_) = a
            w_pz = sZ.n

            def p4_ta(e, h=h_, kc=kc_, j=j_, eslot=eslot_, w_se=w_se_,
                      nvm=nvm_, diag=diag_, first=first_, last_kc=last_,
                      w_pz=w_pz):
                e.wait_ge(sExp.h, w_se)
                if diag:
                    e.wait_ge(vMk.h, nvm)
                if first and w_pz:
                    e.wait_ge(sZ.h, w_pz)
                vb = hbuf(h)[3]
                esl = ebsl(eslot, j)
                e.matmul(psY[:, j:512], vb[:, kc * 64:(kc + 1) * 64],
                         esl, start=first, stop=last_kc)
                e.matmul(psZ[:, j:512], ones64[:, 0:64], esl,
                         start=first, stop=last_kc).then_inc(tZ.h, 1)
            on("tensor", p4_ta)
            tZ.inc()

        def emit_qt_tail(qt_, h_):
            ntz = tZ.n
            w_yst = dSp.n
            w_zr = vY.n

            def p4_sz(e, ntz=ntz, w_zr=w_zr):
                e.wait_ge(tZ.h, ntz)
                if w_zr:
                    e.wait_ge(vY.h, w_zr)
                e.copy(out=zsb[:], in_=psZ[:])
                e.copy(out=ystr[:], in_=psY[:]).then_inc(sZ.h, 1)
            on("scalar", p4_sz)
            sZ.inc()
            nsz = sZ.n

            def p4_vy(e, nsz=nsz, w_yst=w_yst):
                e.wait_ge(sZ.h, nsz)
                if w_yst:
                    e.wait_ge(dSp.h, w_yst)
                e.reciprocal(out=zsb[:], in_=zsb[:])
                e.tensor_tensor(out=yst[:], in0=ystr[:], in1=zsb[:],
                                op=ALU.mult).then_inc(vY.h, 1)
            on("vector", p4_vy)
            vY.inc()
            nvy = vY.n

            def p4_gy(e, nvy=nvy, h=h_, qt=qt_):
                e.wait_ge(vY.h, nvy)
                e.dma_start(out=yd[(h % 2) * 64:(h % 2) * 64 + 64, h // 2,
                                   qt * 512:(qt + 1) * 512],
                            in_=yst[:]).then_inc(dSp.h, 16)
            on("gpsimd", p4_gy)
            dSp.inc(16)

        LAG = 3
        pend = []
        expcnt = {}
        for ci, (h, qt, kc, diag, j, first, last, hlast) in \
                enumerate(allcells):
            seq_last = (ci == len(allcells) - 1)
            sb = 4 + ci % 2
            eslot = ci % 4
            w_exp = sExp.n - 1

            def p4_ts(e, h=h, qt=qt, kc=kc, j=j, sb=sb, n_dh=ndh[h],
                      w_exp=w_exp):
                e.wait_ge(dH4.h, n_dh)
                if w_exp >= 1:
                    e.wait_ge(sExp.h, w_exp)
                qa, ql, kb, _ = hbuf(h)
                ksl = kb[:, kc * 128:(kc + 1) * 128]
                qsl = slice(qt * 512 + j, (qt + 1) * 512)
                e.matmul(ps[sb][:, j:512], ksl, qa[:, qsl],
                         start=True, stop=False)
                e.matmul(ps[sb][:, j:512], ksl, ql[:, qsl],
                         start=False, stop=True).then_inc(tS.h, 1)
            on("tensor", p4_ts)
            tS.inc()
            nts = tS.n
            w_ec = tZ.n

            def p4_se(e, nts=nts, sb=sb, j=j, eslot=eslot, w_ec=w_ec,
                      seq_last=seq_last):
                e.wait_ge(tS.h, nts)
                if w_ec >= 1:
                    e.wait_ge(tZ.h, w_ec)
                act = e.activation(
                    out=ebsl(eslot, j),
                    in_=ps[sb][:, j:512], func=AF.Exp, scale=0.125,
                    bias=cst[:, 3:4])
                if seq_last:
                    sguard(e, sExp.h)
                else:
                    act.then_inc(sExp.h, 1)
            on("scalar", p4_se)
            sExp.inc()
            expcnt[ci] = sExp.n
            nse = sExp.n

            if diag:
                def p4_vm(e, nse=nse, eslot=eslot, j=j):
                    e.wait_ge(sExp.h, nse)
                    sl = ebsl(eslot, j, j + 128)
                    e.tensor_tensor(out=sl, in0=sl, in1=trimask[:],
                                    op=ALU.mult)
                    vguard(e, vMk.h)
                on("vector", p4_vm)
                vMk.inc()
            nvm = vMk.n

            pend.append((ci, h, qt, kc, diag, j, first, last, eslot, nvm,
                         None))
            if len(pend) > LAG:
                a = pend.pop(0)
                emit_ta(a[:10] + (expcnt[a[0] + 1],))
                if a[7]:
                    emit_qt_tail(a[2], a[1])
                    if allcells[a[0]][7]:
                        hh = a[1]
                        if hh + 2 < NH:
                            ndh[hh + 2] = emit_batch(
                                hh + 2, [(tZ.h, tZ.n)])

        while pend:
            a = pend.pop(0)
            w_se = expcnt.get(a[0] + 1, sExp.n)
            emit_ta(a[:10] + (w_se,))
            if a[7]:
                emit_qt_tail(a[2], a[1])

        wo_mm = {}
        for qt in range(NT):
            def p6_sy(e, qt=qt, w_y=dSp.n, w_wb=tWo.n, w_tv=tV.n):
                e.wait_ge(dSp.h, w_y)
                e.wait_ge(tV.h, w_tv)
                if w_wb:
                    e.wait_ge(tWo.h, w_wb)
                wbv = WBv.rearrange("p (k c) -> p k c", k=KC)
                for k in range(KC):
                    e.dma_start(out=wbv[:, k, :],
                                in_=yd[:, k, qt * 512:(qt + 1) * 512]
                                ).then_inc(dY6.h, 16)
            on("sync", p6_sy)
            dY6.inc(16 * KC)
            n_dy = dY6.n

            for mc in range(KC):
                idx = qt * KC + mc
                wsel = wdb if idx % 2 == 0 else eb

                def p6_syw(e, mc=mc, wsel=wsel,
                           w_free=wo_mm.get(idx - 2, 0), w_qk=tQK.n,
                           w_ta=(tZ.n if idx < 2 else 0)):
                    e.wait_ge(tQK.h, w_qk)
                    if w_ta:
                        e.wait_ge(tZ.h, w_ta)
                    if w_free:
                        e.wait_ge(tWo.h, w_free)
                    e.dma_start(out=wsel[:], in_=wot[mc]).then_inc(dW.h, 16)
                on("sync", p6_syw)
                dW.inc(16)
                n_dwo = dW.n
                b = idx % 2
                w_ev = sWo.n - 1

                def p6_t(e, b=b, wsel=wsel, n_dwo=n_dwo, n_dy=n_dy,
                         w_ev=w_ev):
                    e.wait_ge(dW.h, n_dwo)
                    e.wait_ge(dY6.h, n_dy)
                    if w_ev >= 1:
                        e.wait_ge(sWo.h, w_ev)
                    wv_ = wsel.rearrange("p (k c) -> p k c", k=KC)
                    wbv = WBv.rearrange("p (k c) -> p k c", k=KC)
                    last = None
                    for k in range(KC):
                        last = e.matmul(ps[b][:], wv_[:, k, :], wbv[:, k, :],
                                        start=(k == 0), stop=(k == KC - 1))
                    last.then_inc(tWo.h, 1)
                on("tensor", p6_t)
                tWo.inc()
                wo_mm[idx] = tWo.n
                ntw = tWo.n
                w_f = dAF.n

                def p6_s(e, ntw=ntw, b=b, w_f=w_f, idx=idx, w_dsp=dSp.n):
                    e.wait_ge(tWo.h, ntw)
                    if idx < 2:
                        e.wait_ge(dSp.h, w_dsp)
                    if w_f:
                        e.wait_ge(dAF.h, w_f)
                    e.copy(out=fstg[:, b * 512:(b + 1) * 512],
                           in_=ps[b][:]).then_inc(sWo.h, 1)
                on("scalar", p6_s)
                sWo.inc()
                nsw = sWo.n

                def p6_g(e, nsw=nsw, mc=mc, qt=qt, b=b):
                    e.wait_ge(sWo.h, nsw)
                    e.dma_start(out=ad[:, mc, qt * 512:(qt + 1) * 512],
                                in_=fstg[:, b * 512:(b + 1) * 512]
                                ).then_inc(dAF.h, 16)
                on("gpsimd", p6_g)
                dAF.inc(16)

        def p7_sy(e, w_v=vTB.n, w_hl=tH.n):
            if w_hl:
                e.wait_ge(tH.h, w_hl)
            e.wait_ge(vTB.h, w_v)
            yv = YB.rearrange("p (a k c) -> p a k c", a=2, k=KC)
            e.dma_start(out=yv[:, 0], in_=hmt[0]).then_inc(dHM.h, 16)
            e.dma_start(out=yv[:, 1], in_=hmt[1]).then_inc(dHM.h, 16)
        on("sync", p7_sy)
        dHM.inc(32)
        n_hm = dHM.n

        def emit_t2(tc7, nvl):
            w_ev2 = sXq.n

            def p7_t2(e, nvl=nvl, w_ev2=w_ev2):
                e.wait_ge(vL7.h, nvl)
                if w_ev2 >= 1:
                    e.wait_ge(sXq.h, w_ev2)
                last = None
                for j in range(KC):
                    dst = ps[j // 4][:, (j % 4) * 128:(j % 4) * 128 + 128]
                    last = e.matmul(dst, xqtm[:, j * 128:(j + 1) * 128],
                                    identb[:], start=True, stop=True)
                last.then_inc(tXq.h, 1)
            on("tensor", p7_t2)
            tXq.inc()
            ntx = tXq.n
            w_xs = dXQ.n

            def p7_s2(e, ntx=ntx, w_xs=w_xs):
                e.wait_ge(tXq.h, ntx)
                if w_xs:
                    e.wait_ge(dXQ.h, w_xs)
                e.copy(out=xqstg[:, 0:512], in_=ps[0][:])
                e.copy(out=xqstg[:, 512:1024], in_=ps[1][:]).then_inc(sXq.h, 1)
            on("scalar", p7_s2)
            sXq.inc()
            nsx = sXq.n

            def p7_g(e, nsx=nsx, tc=tc7):
                e.wait_ge(sXq.h, nsx)
                xa = xqstg[:, 0:512].rearrange("p (k c) -> p k c", k=4)
                xb_ = xqstg[:, 512:1024].rearrange("p (k c) -> p k c", k=4)
                e.dma_start(out=xqd[:, 0:4, tc * 128:tc * 128 + 128],
                            in_=xa).then_inc(dXQ.h, 16)
                e.dma_start(out=xqd[:, 4:8, tc * 128:tc * 128 + 128],
                            in_=xb_).then_inc(dXQ.h, 16)
            on("gpsimd", p7_g)
            dXQ.inc(32)

        pend7 = None
        for tc in range(TC):
            pa = 2 + (tc % 2) * 2
            w_bank = sQ7.n - 1
            sx = 4 + (tc % 2) * 6

            def p7_t(e, tc=tc, pa=pa, n_hm=n_hm, w_bank=w_bank, w_xn=sP1c.n):
                e.wait_ge(dHM.h, n_hm)
                e.wait_ge(sP1c.h, w_xn)
                if w_bank >= 1:
                    e.wait_ge(sQ7.h, w_bank)
                xv = XN.rearrange("p (k t) -> p k t", k=KC)
                yv = YB.rearrange("p (a k c) -> p a k c", a=2, k=KC)
                last = None
                for half in range(2):
                    for k in range(KC):
                        last = e.matmul(ps[pa + half][:],
                                        xv[:, k, tc * 128:tc * 128 + 128],
                                        yv[:, half, k, :],
                                        start=(k == 0), stop=(k == KC - 1))
                last.then_inc(tH.h, 1)
            on("tensor", p7_t)
            tH.inc()
            nth = tH.n

            if pend7 is not None:
                emit_t2(*pend7)
                pend7 = None

            def p7_v1(e, nth=nth, pa=pa, sx=sx, tc=tc):
                e.wait_ge(tH.h, nth)
                e.tensor_reduce(out=stats[:, 4 * sx:4 * sx + 1], in_=ps[pa][:],
                                axis=AX.X, op=ALU.max,
                                apply_absolute_value=True)
                i0 = e.tensor_reduce(out=stats[:, 4 * sx + 4:4 * sx + 5],
                                     in_=ps[pa + 1][:], axis=AX.X, op=ALU.max,
                                     apply_absolute_value=True)
                vbar(e, i0)
                i1 = e.tensor_tensor(out=stats[:, 4 * sx:4 * sx + 1],
                                     in0=stats[:, 4 * sx:4 * sx + 1],
                                     in1=stats[:, 4 * sx + 4:4 * sx + 5],
                                     op=ALU.max)
                vbar(e, i1)
                i2 = e.tensor_tensor(out=stats[:, 4 * sx:4 * sx + 1],
                                     in0=stats[:, 4 * sx:4 * sx + 1],
                                     in1=cst[:, 6:7], op=ALU.max)
                vbar(e, i2)
                i3 = e.reciprocal(out=stats[:, 4 * sx + 8:4 * sx + 9],
                                  in_=stats[:, 4 * sx:4 * sx + 1])
                vbar(e, i3)
                i4 = e.tensor_tensor(out=stats[:, 4 * sx + 8:4 * sx + 9],
                                     in0=stats[:, 4 * sx + 8:4 * sx + 9],
                                     in1=cst[:, 10:11], op=ALU.mult)
                vbar(e, i4)
                e.reciprocal(out=sc_sb[:, tc:tc + 1],
                             in_=stats[:, 4 * sx + 8:4 * sx + 9])
                vguard(e, vQ7.h)
            on("vector", p7_v1)
            vQ7.inc()
            nvq = vQ7.n
            w_lad = vL7.n

            def p7_s1(e, nvq=nvq, pa=pa, sx=sx, w_lad=w_lad):
                e.wait_ge(vQ7.h, nvq)
                if w_lad:
                    e.wait_ge(vL7.h, w_lad)
                e.activation(out=scr1[:, 0:512], in_=ps[pa][:], func=AF.Identity,
                             scale=stats[:, 4 * sx + 8:4 * sx + 9], bias=cst[:, 4:5])
                e.activation(out=scr1[:, 512:1024], in_=ps[pa + 1][:],
                             func=AF.Identity, scale=stats[:, 4 * sx + 8:4 * sx + 9],
                             bias=cst[:, 4:5]).then_inc(sQ7.h, 1)
            on("scalar", p7_s1)
            sQ7.inc()
            nsq = sQ7.n
            w_xq = tXq.n

            def p7_v2(e, nsq=nsq, w_xq=w_xq, w_do=dOut.n):
                e.wait_ge(sQ7.h, nsq)
                if w_xq:
                    e.wait_ge(tXq.h, w_xq)
                if w_do:
                    e.wait_ge(dOut.h, w_do)
                e.tensor_copy(out=scr2[:], in_=scr1[:])
                for lv in (8.0, 4.0, 2.0, 1.0):
                    e.tensor_scalar(out=obuf[:], in0=scr2[:], scalar1=lv,
                                    scalar2=-lv, op0=ALU.is_ge, op1=ALU.mult)
                    e.tensor_tensor(out=scr2[:], in0=scr2[:], in1=obuf[:],
                                    op=ALU.add)
                e.scalar_tensor_tensor(out=xqtm[:], in0=scr2[:], scalar=8.0,
                                       in1=scr1[:], op0=ALU.add,
                                       op1=ALU.subtract)
                vguard(e, vL7.h)
            on("vector", p7_v2)
            vL7.inc()
            pend7 = (tc, vL7.n)
        emit_t2(*pend7)
        pend7 = None

        for qt in range(NT):
            def p7b_syx(e, qt=qt, w_xq=dXQ.n, w_cons=tFf.n):
                e.wait_ge(dXQ.h, w_xq)
                if w_cons:
                    e.wait_ge(tFf.h, w_cons)
                e.dma_start(out=xqb.rearrange("p (k c) -> p k c", k=KC),
                            in_=xqd[:, 0:KC, qt * 512:(qt + 1) * 512]
                            ).then_inc(dXB.h, 16)
            on("sync", p7b_syx)
            dXB.inc(16)
            n_xb = dXB.n

            for mc in range(KC):
                idx = qt * KC + mc
                slot = idx % 2

                def p7b_syn(e, mc=mc, slot=slot, w_tn=dTN.n,
                            w_cons=tFf.n - 1,
                            w_ts=(tS.n if idx < 2 else 0)):
                    e.wait_ge(dTN.h, w_tn)
                    if w_ts:
                        e.wait_ge(tS.h, w_ts)
                    if w_cons >= 1:
                        e.wait_ge(tFf.h, w_cons)
                    dst = ntb[:, slot * KC * 128:(slot + 1) * KC * 128]
                    e.dma_start(out=dst.rearrange("p (k c) -> p k c", k=KC),
                                in_=ternd[:, 0:KC, mc * 128:mc * 128 + 128]
                                ).then_inc(dNT.h, 16)
                on("sync", p7b_syn)
                dNT.inc(16)
                n_nt = dNT.n

                def p7b_rly(e, n_nt=n_nt, n_xb=n_xb):
                    e.wait_ge(dNT.h, n_nt)
                    e.wait_ge(dXB.h, n_xb)
                    e.tensor_tensor(out=dumw[:], in0=dumw[:], in1=dumw[:],
                                    op=ALU.max).then_inc(dRL.h, 1)
                on("vector", p7b_rly)
                dRL.inc()
                n_rl = dRL.n
                b = idx % 2
                w_ev = sFf.n - 1

                def p7b_t(e, b=b, slot=slot, n_rl=n_rl,
                          w_ev=w_ev, idx=idx, w_ff0=(sXq.n, sWo.n)):
                    e.wait_ge(dRL.h, n_rl)
                    if w_ev >= 1:
                        e.wait_ge(sFf.h, w_ev)
                    if idx % (NT * KC) < 2:
                        e.wait_ge(sXq.h, w_ff0[0])
                    ntv = ntb[:, slot * KC * 128:(slot + 1) * KC * 128
                              ].rearrange("p (k c) -> p k c", k=KC)
                    xqv = xqb.rearrange("p (k c) -> p k c", k=KC)
                    last = None
                    for k in range(KC):
                        last = e.matmul(ps[b][:], ntv[:, k, :], xqv[:, k, :],
                                        start=(k == 0), stop=(k == KC - 1))
                    last.then_inc(tFf.h, 1)
                on("tensor", p7b_t)
                tFf.inc()
                ntf = tFf.n
                w_f = dAF.n

                def p7b_s(e, ntf=ntf, b=b, mc=mc, w_f=w_f):
                    e.wait_ge(tFf.h, ntf)
                    if w_f:
                        e.wait_ge(dAF.h, w_f)
                    e.activation(out=fstg[:, b * 512:(b + 1) * 512],
                                 in_=ps[b][:], func=AF.Identity,
                                 scale=so_sb[:, mc:mc + 1]).then_inc(sFf.h, 1)
                on("scalar", p7b_s)
                sFf.inc()
                nsf = sFf.n

                def p7b_g(e, nsf=nsf, mc=mc, qt=qt, b=b):
                    e.wait_ge(sFf.h, nsf)
                    e.dma_start(out=fd[:, mc, qt * 512:(qt + 1) * 512],
                                in_=fstg[:, b * 512:(b + 1) * 512]
                                ).then_inc(dAF.h, 16)
                on("gpsimd", p7b_g)
                dAF.inc(16)

        p8_tt = {}
        p8_vv = {}
        p8_gg = {}
        for tc in range(TC):
            xd_ = xdb if tc % 2 == 0 else vh
            s1_ = scr1 if tc % 2 == 0 else XN[:, 0:1024]
            s2_ = scr2 if tc % 2 == 0 else XN[:, 1024:2048]
            stg_ = stg if tc % 2 == 0 else XN[:, 2048:3072]
            fa_ = fstg[:, 0:512] if tc % 2 == 0 else XN[:, 3072:3584]
            fb_ = fstg[:, 512:1024] if tc % 2 == 0 else XN[:, 3584:4096]

            def p8_sy(e, tc=tc, s=s, xd_=xd_, s1_=s1_, s2_=s2_,
                      w_cons=p8_vv.get(tc - 2, 0), w_all=dAF.n,
                      w_tr=(p8_tt.get(tc - 2, 0) if tc >= 2 else tTr.n),
                      w_do=(p8_gg.get(tc - 2, 0) if tc >= 2 else dOut.n),
                      row0=row0, w_sp1=sP1b.n):
                e.wait_ge(dAF.h, w_all)
                e.wait_ge(sP1b.h, w_sp1)
                if w_cons >= 1:
                    e.wait_ge(vO.h, w_cons)
                e.dma_start(out=xd_[:],
                            in_=xin[row0 + tc * 128: row0 + tc * 128 + 128, :]
                            ).then_inc(dX.h, 16)
                if w_tr:
                    e.wait_ge(tTr.h, w_tr)
                if w_do:
                    e.wait_ge(dOut.h, w_do)
                e.dma_start(out=s1_[:],
                            in_=ad[:, 0:KC, tc * 128:tc * 128 + 128]
                            ).then_inc(dX.h, 16)
                e.dma_start(out=s2_[:],
                            in_=fd[:, 0:KC, tc * 128:tc * 128 + 128]
                            ).then_inc(dX.h, 16)
            on("sync", p8_sy)
            dX.inc(48)
            n_d8 = dX.n
            w_ps = sTr.n

            def p8_rly(e, n_d8=n_d8):
                e.wait_ge(dX.h, n_d8)
                e.tensor_tensor(out=dumw[:], in0=dumw[:], in1=dumw[:],
                                op=ALU.max).then_inc(dRL.h, 1)
            on("vector", p8_rly)
            dRL.inc()
            n_rl8 = dRL.n

            def p8_t(e, n_rl8=n_rl8, s1_=s1_, s2_=s2_, w_ps=w_ps,
                     w_ff=sFf.n):
                e.wait_ge(dRL.h, n_rl8)
                e.wait_ge(sFf.h, w_ff)
                if w_ps:
                    e.wait_ge(sTr.h, w_ps)
                last = None
                for j in range(KC):
                    dst = ps[j // 4][:, (j % 4) * 128:(j % 4) * 128 + 128]
                    last = e.matmul(dst, s1_[:, j * 128:(j + 1) * 128],
                                    identf[:], start=True, stop=True)
                for j in range(KC):
                    dst = ps[2 + j // 4][:, (j % 4) * 128:(j % 4) * 128 + 128]
                    last = e.matmul(dst, s2_[:, j * 128:(j + 1) * 128],
                                    identf[:], start=True, stop=True)
                last.then_inc(tTr.h, 1)
            on("tensor", p8_t)
            tTr.inc()
            ntt = tTr.n
            p8_tt[tc] = tTr.n

            def p8_s(e, ntt=ntt, tc=tc, stg_=stg_, fa_=fa_, fb_=fb_,
                     w_dsp=dSp.n, w_vp=p8_vv.get(tc - 2, 0)):
                e.wait_ge(tTr.h, ntt)
                if tc == 0:
                    e.wait_ge(dSp.h, w_dsp)
                if w_vp:
                    e.wait_ge(vO.h, w_vp)
                e.copy(out=stg_[:, 0:512], in_=ps[0][:])
                e.copy(out=stg_[:, 512:1024], in_=ps[1][:])
                e.activation(out=fa_, in_=ps[2][:],
                             func=AF.Identity, scale=sc_sb[:, tc:tc + 1])
                e.activation(out=fb_, in_=ps[3][:],
                             func=AF.Identity,
                             scale=sc_sb[:, tc:tc + 1]).then_inc(sTr.h, 1)
            on("scalar", p8_s)
            sTr.inc()
            nst = sTr.n

            def p8_v(e, nst=nst, tc=tc, xd_=xd_, s1_=s1_, stg_=stg_,
                     fa_=fa_, fb_=fb_):
                e.wait_ge(sTr.h, nst)
                e.tensor_tensor(out=s1_[:, 0:512], in0=stg_[:, 0:512],
                                in1=fa_, op=ALU.add)
                e.tensor_tensor(out=s1_[:, 512:1024], in0=stg_[:, 512:1024],
                                in1=fb_, op=ALU.add)
                e.tensor_tensor(out=s1_[:], in0=s1_[:], in1=xd_[:],
                                op=ALU.add).then_inc(vO.h, 1)
            on("vector", p8_v)
            vO.inc()
            nvo = vO.n
            p8_vv[tc] = vO.n

            def p8_g(e, nvo=nvo, tc=tc, s1_=s1_, row0=row0):
                e.wait_ge(vO.h, nvo)
                e.dma_start(out=out[row0 + tc * 128: row0 + tc * 128 + 128, :],
                            in_=s1_[:]).then_inc(dOut.h, 16)
            on("gpsimd", p8_g)
            dOut.inc(16)
            p8_gg[tc] = dOut.n

    def fin(e, w=dOut.n):
        e.wait_ge(dOut.h, w)
    on("gpsimd", fin)

    with nc.Block() as block:
        for eng in ("sync", "gpsimd", "tensor", "vector", "scalar"):
            fns = ops[eng]

            def make(fns=fns):
                def section(e):
                    for fn in fns:
                        fn(e)
                return section
            getattr(block, eng)(make())

    return nc


def _hadamard(size):
    h = np.array([[1.0]], dtype=np.float64)
    while h.shape[0] < size:
        h = np.block([[h, h], [h, -h]])
    return (h / np.sqrt(size)).astype(np.float32)


def _host_inputs(L0_attn_W_qkv, L0_attn_W_o, L0_ffn_weight):
    Wqkv = np.ascontiguousarray(np.asarray(L0_attn_W_qkv, dtype=np.float32))
    Wo = np.ascontiguousarray(np.asarray(L0_attn_W_o, dtype=np.float32))
    Wf = np.ascontiguousarray(np.asarray(L0_ffn_weight, dtype=np.float32))

    perm64 = np.empty(64, np.int64)
    perm64[0::2] = np.arange(32)
    perm64[1::2] = np.arange(32, 64)
    colperm = np.arange(3072)
    for h in range(NH):
        colperm[64 * h:64 * h + 64] = 64 * h + perm64
        colperm[1024 + 64 * h:1024 + 64 * h + 64] = 1024 + 64 * h + perm64
    Wp = Wqkv[:, colperm]

    wqk = np.ascontiguousarray(
        Wp[:, :2048].reshape(KC, 128, 16, 128).transpose(2, 1, 0, 3))
    wvt = np.ascontiguousarray(
        Wp[:, 2048:].reshape(KC, 128, 2, 512).transpose(2, 1, 0, 3))
    wot = np.ascontiguousarray(
        Wo.reshape(KC, 128, KC, 128).transpose(2, 1, 0, 3))
    wft = np.ascontiguousarray(Wf.reshape(KC, 128, D))
    H = _hadamard(D)
    hmt = np.ascontiguousarray(
        H.reshape(KC, 128, 2, 512).transpose(2, 1, 0, 3))
    iota = np.ascontiguousarray(
        np.broadcast_to(np.arange(T, dtype=np.float32), (128, T)))
    f = (np.arange(128) % 64) // 2
    theta = 1.0 / (10000.0 ** (f.astype(np.float64) / 32.0))
    th = theta.astype(np.float32)
    tl = (theta - th.astype(np.float64)).astype(np.float32)
    csf = np.where(np.arange(128) % 2 == 0, np.float32(-1.0), np.float32(1.0))
    ones = np.ones(128, np.float32)
    cols = [th, tl, csf.astype(np.float32),
            -12.0 * ones, 8.5 * ones,
            np.float32(2.0 ** -10) * ones, np.float32(1e-5) * ones,
            0.5 * ones, -1.0 * ones,
            np.float32(np.float32(1.0) / np.float32(7.0)) * ones,
            7.0 * ones, 0.0 * ones]
    thet = np.ascontiguousarray(np.stack(cols, axis=1).astype(np.float32))
    return wqk, wvt, wot, wft, hmt, iota, thet


def kernel(L0_attn_x, L0_attn_W_qkv, L0_attn_W_o, L0_ffn_weight):
    x = np.ascontiguousarray(np.asarray(L0_attn_x, dtype=np.float32))
    wqk, wvt, wot, wft, hmt, iota, thet = _host_inputs(
        L0_attn_W_qkv, L0_attn_W_o, L0_ffn_weight)

    nc = build_nc()
    in_maps = []
    for c in range(8):
        xs = np.ascontiguousarray(x[2 * c:2 * c + 2].reshape(NTOK, D))
        in_maps.append({"xin": xs, "wqk": wqk, "wvt": wvt, "wot": wot,
                        "wft": wft, "hmt": hmt, "iota": iota, "thet": thet})
    res = run_bass_kernel_spmd(nc, in_maps, core_ids=list(range(8)))
    outs = [np.asarray(r["out"]) for r in res.results]
    full = np.stack(outs, axis=0).reshape(16, T, D)
    return full


# version 16 (final)
# speedup vs baseline: 1.9416x; 1.0181x over previous
import math
import numpy as np
import ml_dtypes
import concourse.bass as bass
import concourse.mybir as mybir
from concourse.bass_utils import run_bass_kernel_spmd

F32 = mybir.dt.float32
BF16 = mybir.dt.bfloat16
AF = mybir.ActivationFunctionType
ALU = mybir.AluOpType
AX = mybir.AxisListType

T = 2048
D = 1024
NH = 16
DH = 64
NSEQ = 2
NTOK = NSEQ * T
KC = 8
TC = T // 128
NT = T // 512
B_EXP = 12.0


class Sem:
    def __init__(self, nc, name):
        self.h = nc.alloc_semaphore(name)
        self.n = 0

    def inc(self, step=1):
        self.n += step
        return self.n


def build_nc(debug=False):
    nc = bass.Bass("TRN2", target_bir_lowering=False,
                   detect_race_conditions=False)
    ikind = "ExternalOutput" if debug else "Internal"

    xin = nc.dram_tensor("xin", [NTOK, D], F32, kind="ExternalInput").ap()
    wqk = nc.dram_tensor("wqk", [16, 128, KC, 128], F32, kind="ExternalInput").ap()
    wvt = nc.dram_tensor("wvt", [2, 128, KC, 512], F32, kind="ExternalInput").ap()
    wot = nc.dram_tensor("wot", [KC, 128, KC, 128], F32, kind="ExternalInput").ap()
    wft = nc.dram_tensor("wft", [KC, 128, D], F32, kind="ExternalInput").ap()
    hmt = nc.dram_tensor("hmt", [2, 128, KC, 512], F32, kind="ExternalInput").ap()
    iota = nc.dram_tensor("iota", [128, T], F32, kind="ExternalInput").ap()
    thet = nc.dram_tensor("thet", [128, 12], F32, kind="ExternalInput").ap()
    out = nc.dram_tensor("out", [NTOK, D], F32, kind="ExternalOutput").ap()

    qd = nc.dram_tensor("qd", [2, KC, 128, T], BF16, kind=ikind).ap()
    kd = nc.dram_tensor("kd", [2, KC, 128, T], BF16, kind=ikind).ap()
    vd = nc.dram_tensor("vd", [128, TC, D], F32, kind=ikind).ap()
    yd = nc.dram_tensor("yd", [128, KC, T], F32, kind=ikind).ap()
    ad = nc.dram_tensor("ad", [128, KC, T], F32, kind=ikind).ap()
    fd = nc.dram_tensor("fd", [128, KC, T], F32, kind=ikind).ap()
    xqd = nc.dram_tensor("xqd", [128, KC, T], BF16, kind=ikind).ap()
    ternd = nc.dram_tensor("ternd", [128, KC, D], BF16, kind=ikind).ap()
    if debug:
        xnd_dbg = nc.dram_tensor("xnd_dbg", [128, KC * T], F32,
                                 kind="ExternalOutput").ap()
        st_dbg = nc.dram_tensor("st_dbg", [128, 64], F32,
                                kind="ExternalOutput").ap()
        sb_dbg = nc.dram_tensor("sb_dbg", [128, KC + TC], F32,
                                kind="ExternalOutput").ap()

    A = nc.alloc_sbuf_tensor
    XN = A("XN", [128, KC * T], F32).ap()
    YB = A("YB", [128, KC * D], F32).ap()
    WBv = A("WBv", [128, KC * 512], F32).ap()

    ctab = A("ctab", [128, T], F32).ap()
    stab = A("stab", [128, T], F32).ap()
    so_sb = A("so_sb", [128, KC], F32).ap()
    sc_sb = A("sc_sb", [128, TC], F32).ap()
    trimask = A("trimask", [128, 128], F32).ap()
    identf = A("identf", [128, 128], F32).ap()
    identb = A("identb", [128, 128], BF16).ap()
    ones64 = A("ones64", [128, 64], F32).ap()
    cst = A("cst", [128, 12], F32).ap()
    stats = A("stats", [128, 64], F32).ap()
    dum = A("dum", [128, 8], F32).ap()
    dumw = A("dumw", [128, 384], F32).ap()

    wdb = A("wdb", [128, KC * 128], F32).ap()
    xdb = A("xdb", [128, D], F32).ap()
    stg = A("stg", [128, 2 * 512], F32).ap()
    yst = stg[0:64, 0:512]
    qha = A("qha", [128, T], BF16).ap()
    qlb = A("qlb", [128, T], BF16).ap()
    khl = A("khl", [128, T], BF16).ap()
    vh = A("vh", [128, TC * DH], F32).ap()
    eb = A("eb", [128, 2 * 512], F32).ap()
    zsb = A("zsb", [64, 512], F32).ap()
    ystr = A("ystr", [64, 512], F32).ap()
    scr1 = A("scr1", [128, D], F32).ap()
    scr2 = A("scr2", [128, D], F32).ap()
    xqtm = A("xqtm", [128, D], BF16).ap()
    obuf = A("obuf", [128, D], F32).ap()
    fstg = A("fstg", [128, 2 * 512], F32).ap()
    xqb = A("xqb", [128, KC * 512], BF16).ap()
    ntb = A("ntb", [128, 2 * KC * 128], BF16).ap()
    xqstg = A("xqstg", [128, 2 * 512], BF16).ap()
    ntstg = A("ntstg", [128, KC * 128], BF16).ap()

    qha2 = xqb[:, 0:T]
    qlb2 = xqb[:, T:2 * T]
    khl2 = ntb[:, 0:T]
    vh2 = obuf

    def ebsl(eslot, j, end=512):
        if eslot < 2:
            base = eslot * 512
            return eb[:, base + j:base + end]
        base = (eslot - 2) * 512
        return fstg[:, base + j:base + end]

    P = nc.alloc_psum_tensor
    ps = [P(f"ps{i}", [128, 512], F32).ap() for i in range(6)]
    psY = P("psY", [64, 512], F32).ap()
    psZ = P("psZ", [64, 512], F32).ap()

    ops = {e: [] for e in ("sync", "gpsimd", "tensor", "vector", "scalar")}

    def on(eng, fn):
        ops[eng].append(fn)

    sems = {}

    def sem(name):
        if name not in sems:
            sems[name] = Sem(nc, name)
        return sems[name]

    tpi = 2.0 * math.pi
    c2ph = float(np.float32(tpi)); c2pl = float(np.float32(tpi - float(np.float32(tpi))))
    cph = float(np.float32(math.pi)); cpl = float(np.float32(math.pi - float(np.float32(math.pi))))
    i2p = 1.0 / tpi
    ci2h = float(np.float32(i2p)); ci2l = float(np.float32(i2p - float(np.float32(i2p))))
    p2 = math.pi / 2
    cp2h = float(np.float32(p2)); cp2l = float(np.float32(p2 - float(np.float32(p2))))
    CLAMP = 3.1415925
    C7 = float(np.float32(1.0 / 7.0))
    INV1024 = float(np.float32(2.0 ** -10))

    idf_np = np.eye(128, dtype=np.float32)
    idb_np = np.eye(128, dtype=ml_dtypes.bfloat16).view(np.uint16)
    tri_np = (np.arange(128)[:, None] <= np.arange(128)[None, :]).astype(np.float32)
    t_idf = nc.inline_tensor(idf_np, "idf")
    t_idb = nc.inline_tensor(idb_np, "idb")
    t_tri = nc.inline_tensor(tri_np, "tri")

    dTB = sem("dTB"); vTB = sem("vTB"); sTB = sem("sTB"); tTB = sem("tTB")
    dTN = sem("dTN")

    def tb_load(e):
        e.dma_start(out=ctab[:], in_=iota).then_inc(dTB.h, 16)
        e.dma_start(out=cst[:], in_=thet).then_inc(dTB.h, 16)
        e.dma_start(out=identf[:], in_=t_idf.ap()).then_inc(dTB.h, 16)
        e.dma_start(out=identb[:], in_=t_idb.ap().bitcast(BF16)).then_inc(dTB.h, 16)
        e.dma_start(out=trimask[:], in_=t_tri.ap()).then_inc(dTB.h, 16)
    on("sync", tb_load)
    dTB.inc(16 * 5)
    n_tbload = dTB.n

    ANG = YB[:, 0:T]
    WRK = YB[:, T:2 * T]
    WK2 = YB[:, 2 * T:3 * T]
    RR = YB[:, 3 * T:4 * T]

    for is_cos in (False, True):
        def tab_v1(e, is_cos=is_cos, w=n_tbload):
            e.wait_ge(dTB.h, w)
            e.tensor_scalar(out=ANG, in0=ctab[:], scalar1=cst[:, 0:1],
                            scalar2=None, op0=ALU.mult)
            e.tensor_scalar(out=WRK, in0=ctab[:], scalar1=cst[:, 1:2],
                            scalar2=None, op0=ALU.mult)
            e.tensor_tensor(out=ANG, in0=ANG, in1=WRK, op=ALU.add)
            if is_cos:
                e.tensor_scalar(out=ANG, in0=ANG, scalar1=cp2h, scalar2=cp2l,
                                op0=ALU.add, op1=ALU.add)
            e.tensor_scalar(out=WRK, in0=ANG, scalar1=ci2h, scalar2=None,
                            op0=ALU.mult)
            e.tensor_scalar(out=WK2, in0=ANG, scalar1=ci2l, scalar2=None,
                            op0=ALU.mult)
            e.tensor_tensor(out=WRK, in0=WRK, in1=WK2, op=ALU.add)
            e.tensor_scalar(out=WRK, in0=WRK, scalar1=0.5, scalar2=None,
                            op0=ALU.add)
            e.tensor_copy(out=WK2, in_=WRK)
            for lv in (256., 128., 64., 32., 16., 8., 4., 2., 1.):
                e.tensor_scalar(out=RR, in0=WRK, scalar1=lv, scalar2=-lv,
                                op0=ALU.is_ge, op1=ALU.mult)
                e.tensor_tensor(out=WRK, in0=WRK, in1=RR, op=ALU.add)
            e.tensor_tensor(out=WK2, in0=WK2, in1=WRK, op=ALU.subtract)
            e.scalar_tensor_tensor(out=RR, in0=WK2, scalar=-c2ph, in1=ANG,
                                   op0=ALU.mult, op1=ALU.add)
            e.scalar_tensor_tensor(out=WRK, in0=WK2, scalar=-c2pl, in1=RR,
                                   op0=ALU.mult, op1=ALU.add)
            e.tensor_scalar(out=WRK, in0=WRK, scalar1=CLAMP, scalar2=-CLAMP,
                            op0=ALU.min, op1=ALU.max).then_inc(vTB.h, 1)
        on("vector", tab_v1)
        vTB.inc()
        nv = vTB.n

        def tab_s(e, nv=nv):
            e.wait_ge(vTB.h, nv)
            e.activation(out=RR, in_=WRK, func=AF.Sin).then_inc(sTB.h, 1)
        on("scalar", tab_s)
        sTB.inc()
        ns = sTB.n

        def tab_v2(e, is_cos=is_cos, ns=ns):
            e.wait_ge(sTB.h, ns)
            if is_cos:
                e.tensor_tensor(out=ctab[:], in0=RR, in1=RR,
                                op=ALU.max).then_inc(vTB.h, 1)
            else:
                e.tensor_scalar(out=stab[:], in0=RR, scalar1=cst[:, 2:3],
                                scalar2=None, op0=ALU.mult).then_inc(vTB.h, 1)
        on("vector", tab_v2)
        vTB.inc()

    def tb_consts(e):
        e.memset(ones64[:], 1.0)
        e.memset(dumw[:], 0.0)
        e.memset(stats[:], 0.0)
        e.memset(dum[:], 0.0)
        e.memset(sc_sb[:], 0.0)
        e.memset(so_sb[:], 0.0)
        e.memset(qlb[64:128, :], 0.0)
    on("vector", tb_consts)

    for oc in range(KC):
        w_t = tTB.n

        def f_sy(e, oc=oc, w_t=w_t):
            if oc > 0:
                e.wait_ge(tTB.h, w_t)
            e.dma_start(out=WBv[:, 0:D], in_=wft[oc]).then_inc(dTB.h, 16)
        on("sync", f_sy)
        dTB.inc(16)
        n_wf = dTB.n

        def f_v(e, oc=oc, n_wf=n_wf):
            e.wait_ge(dTB.h, n_wf)
            wt_ = WBv[:, 0:D]
            i0 = e.tensor_reduce(out=stats[:, 16:17], in_=wt_, axis=AX.X,
                                 op=ALU.add, apply_absolute_value=True)
            vbar(e, i0)
            i1 = e.tensor_tensor(out=stats[:, 16:17], in0=stats[:, 16:17],
                                 in1=cst[:, 5:6], op=ALU.mult)
            vbar(e, i1)
            i2 = e.tensor_tensor(out=stats[:, 16:17], in0=stats[:, 16:17],
                                 in1=cst[:, 6:7], op=ALU.max)
            vbar(e, i2)
            e.tensor_tensor(out=so_sb[:, oc:oc + 1], in0=stats[:, 16:17],
                            in1=stats[:, 16:17], op=ALU.max)
            i3 = e.tensor_tensor(out=stats[:, 20:21], in0=stats[:, 16:17],
                                 in1=cst[:, 7:8], op=ALU.mult)
            vbar(e, i3)
            i4 = e.tensor_tensor(out=stats[:, 24:25], in0=stats[:, 20:21],
                                 in1=cst[:, 8:9], op=ALU.mult)
            vbar(e, i4)
            e.tensor_scalar(out=scr1[:], in0=wt_, scalar1=stats[:, 20:21],
                            scalar2=None, op0=ALU.is_gt)
            e.tensor_scalar(out=scr2[:], in0=wt_, scalar1=stats[:, 24:25],
                            scalar2=None, op0=ALU.is_lt)
            e.tensor_tensor(out=xqtm[:], in0=scr2[:], in1=scr1[:],
                            op=ALU.subtract).then_inc(vTB.h, 1)
        on("vector", f_v)
        vTB.inc()
        n_vt = vTB.n
        w_s = sTB.n

        def f_t(e, n_vt=n_vt, w_s=w_s, oc=oc):
            e.wait_ge(vTB.h, n_vt)
            if oc > 0:
                e.wait_ge(sTB.h, w_s)
            last = None
            for j in range(KC):
                dst = ps[j // 4][:, (j % 4) * 128:(j % 4) * 128 + 128]
                last = e.matmul(dst, xqtm[:, j * 128:(j + 1) * 128], identb[:],
                                start=True, stop=True)
            last.then_inc(tTB.h, 1)
        on("tensor", f_t)
        tTB.inc()
        n_tt = tTB.n
        w_dn = dTN.n

        def f_s(e, oc=oc, n_tt=n_tt, w_dn=w_dn):
            e.wait_ge(tTB.h, n_tt)
            if w_dn:
                e.wait_ge(dTN.h, w_dn)
            last = None
            for j in range(KC):
                src = ps[j // 4][:, (j % 4) * 128:(j % 4) * 128 + 128]
                last = e.copy(out=ntstg[:, j * 128:(j + 1) * 128], in_=src)
            last.then_inc(sTB.h, 1)
        on("scalar", f_s)
        sTB.inc()
        n_fs = sTB.n

        def f_g(e, oc=oc, n_fs=n_fs):
            e.wait_ge(sTB.h, n_fs)
            nv_ = ntstg.rearrange("p (k c) -> p k c", k=KC)
            e.dma_start(out=ternd[:, 0:KC, oc * 128:oc * 128 + 128],
                        in_=nv_).then_inc(dTN.h, 16)
        on("gpsimd", f_g)
        dTN.inc(16)

    TBv, TBs, TBt, TBd = vTB.n, sTB.n, tTB.n, dTB.n

    dX = sem("dX"); dW = sem("dW"); dSp = sem("dSp"); dH4 = sem("dH4")
    dY6 = sem("dY6"); dAF = sem("dAF"); dOut = sem("dOut"); dHM = sem("dHM")
    dXQ = sem("dXQ"); dNT = sem("dNT"); dXB = sem("dXB")
    dRL = sem("dRL")
    tP1 = sem("tP1"); tQK = sem("tQK"); tV = sem("tV"); tS = sem("tS")
    tZ = sem("tZ"); tWo = sem("tWo"); tH = sem("tH"); tXq = sem("tXq")
    tTr = sem("tTr"); tFf = sem("tFf")
    sP1a = sem("sP1a"); sP1b = sem("sP1b"); sP1c = sem("sP1c")
    sP3 = sem("sP3"); sV3 = sem("sV3"); sExp = sem("sExp"); sZ = sem("sZ")
    sWo = sem("sWo"); sQ7 = sem("sQ7"); sXq = sem("sXq"); sTr = sem("sTr")
    sFf = sem("sFf")
    vLN1 = sem("vLN1"); vLN2 = sem("vLN2"); vR3 = sem("vR3"); vMk = sem("vMk")
    vY = sem("vY"); vQ7 = sem("vQ7"); vL7 = sem("vL7"); vO = sem("vO")

    vB = sem("vB")

    def vbar(e, inst):
        inst.then_inc(vB.h, 1)
        e.wait_ge(vB.h, vB.inc())

    def vguard(e, semh):
        e.tensor_tensor(out=dumw[:], in0=dumw[:], in1=dumw[:],
                        op=ALU.max).then_inc(semh, 1)

    sdmw = A("sdmw", [128, 448], F32).ap()

    def sguard(e, semh):
        e.copy(out=sdmw[:, 0:384], in_=dumw[:]).then_inc(semh, 1)

    for s in range(NSEQ):
        row0 = s * T

        for tc in range(TC):
            def p1_sy(e, tc=tc, s=s, w=sP1b.n, row0=row0, w_vo=vO.n):
                if s > 0 or tc > 0:
                    e.wait_ge(sP1b.h, w)
                if s > 0:
                    e.wait_ge(vO.h, w_vo)
                e.dma_start(out=xdb[:],
                            in_=xin[row0 + tc * 128: row0 + tc * 128 + 128, :]
                            ).then_inc(dX.h, 16)
            on("sync", p1_sy)
            dX.inc(16)
            n_dx = dX.n

            def p1_v1(e, n_dx=n_dx, w=TBv, w_sq7=sQ7.n):
                e.wait_ge(dX.h, n_dx)
                e.wait_ge(vTB.h, w)
                if w_sq7:
                    e.wait_ge(sQ7.h, w_sq7)
                i0 = e.tensor_reduce(out=stats[:, 16:17], in_=xdb[:],
                                     axis=AX.X, op=ALU.add)
                vbar(e, i0)
                e.tensor_tensor(out=stats[:, 20:21], in0=stats[:, 16:17],
                                in1=cst[:, 5:6],
                                op=ALU.mult).then_inc(vLN1.h, 1)
            on("vector", p1_v1)
            vLN1.inc()
            nv1 = vLN1.n

            def p1_s1(e, n_dx=n_dx, nv1=nv1, w=TBs, w_ttr=tTr.n):
                e.wait_ge(dX.h, n_dx)
                e.wait_ge(sTB.h, w)
                if w_ttr:
                    e.wait_ge(tTr.h, w_ttr)
                e.activation(out=scr2[:], in_=xdb[:], func=AF.Square,
                             accum_out=stats[:, 24:25])
                sguard(e, sP1a.h)
            on("scalar", p1_s1)
            sP1a.inc()
            ns1 = sP1a.n

            def p1_v2(e, nv1=nv1, ns1=ns1):
                e.wait_ge(sP1a.h, ns1)
                e.wait_ge(vLN1.h, nv1)
                e.tensor_tensor(out=stats[:, 28:29], in0=stats[:, 20:21],
                                in1=stats[:, 20:21], op=ALU.mult)
                i1 = e.tensor_tensor(out=stats[:, 32:33], in0=stats[:, 24:25],
                                     in1=cst[:, 5:6], op=ALU.mult)
                vbar(e, i1)
                i2 = e.tensor_tensor(out=stats[:, 32:33], in0=stats[:, 32:33],
                                     in1=stats[:, 28:29], op=ALU.subtract)
                vbar(e, i2)
                e.tensor_tensor(out=stats[:, 32:33], in0=stats[:, 32:33],
                                in1=cst[:, 6:7], op=ALU.add)
                vguard(e, vLN1.h)
            on("vector", p1_v2)
            vLN1.inc()
            nv2 = vLN1.n

            def p1_s2(e, nv2=nv2):
                e.wait_ge(vLN1.h, nv2)
                e.activation(out=stats[:, 36:37], in_=stats[:, 32:33],
                             func=AF.Sqrt)
                sguard(e, sP1a.h)
            on("scalar", p1_s2)
            sP1a.inc()
            ns2 = sP1a.n

            def p1_v3(e, ns2=ns2):
                e.wait_ge(sP1a.h, ns2)
                i0 = e.reciprocal(out=stats[:, 40:41], in_=stats[:, 36:37])
                vbar(e, i0)
                i1 = e.tensor_tensor(out=stats[:, 44:45], in0=stats[:, 32:33],
                                     in1=stats[:, 40:41], op=ALU.mult)
                vbar(e, i1)
                i2 = e.tensor_tensor(out=stats[:, 44:45], in0=stats[:, 36:37],
                                     in1=stats[:, 44:45], op=ALU.add)
                vbar(e, i2)
                i3 = e.tensor_tensor(out=stats[:, 44:45], in0=stats[:, 44:45],
                                     in1=cst[:, 7:8], op=ALU.mult)
                vbar(e, i3)
                i4 = e.reciprocal(out=stats[:, 48:49], in_=stats[:, 44:45])
                vbar(e, i4)
                i5 = e.tensor_tensor(out=stats[:, 52:53], in0=stats[:, 20:21],
                                     in1=stats[:, 48:49], op=ALU.mult)
                vbar(e, i5)
                e.tensor_tensor(out=stats[:, 52:53], in0=stats[:, 52:53],
                                in1=cst[:, 8:9], op=ALU.mult)
                vguard(e, vLN2.h)
            on("vector", p1_v3)
            vLN2.inc()
            nv3 = vLN2.n
            w_tp1 = tP1.n

            def p1_s3(e, nv3=nv3, w_tp1=w_tp1, w_ttr=tTr.n, w_do=dOut.n):
                e.wait_ge(vLN2.h, nv3)
                if w_tp1:
                    e.wait_ge(tP1.h, w_tp1)
                if w_ttr:
                    e.wait_ge(tTr.h, w_ttr)
                if w_do:
                    e.wait_ge(dOut.h, w_do)
                e.activation(out=scr1[:], in_=xdb[:], func=AF.Identity,
                             scale=stats[:, 48:49],
                             bias=stats[:, 52:53]).then_inc(sP1b.h, 1)
            on("scalar", p1_s3)
            sP1b.inc()
            ns3 = sP1b.n
            w_ev = sP1c.n

            def p1_t(e, ns3=ns3, w_ev=w_ev, w_tb=TBs):
                e.wait_ge(sP1b.h, ns3)
                e.wait_ge(sTB.h, w_tb)
                if w_ev:
                    e.wait_ge(sP1c.h, w_ev)
                last = None
                for j in range(KC):
                    dst = ps[j // 4][:, (j % 4) * 128:(j % 4) * 128 + 128]
                    last = e.matmul(dst, scr1[:, j * 128:(j + 1) * 128],
                                    identf[:], start=True, stop=True)
                last.then_inc(tP1.h, 1)
            on("tensor", p1_t)
            tP1.inc()
            nt1 = tP1.n

            def p1_s4(e, nt1=nt1, tc=tc):
                e.wait_ge(tP1.h, nt1)
                xv = XN.rearrange("p (k t) -> p k t", k=KC)
                e.copy(out=xv[:, 0:4, tc * 128:tc * 128 + 128],
                       in_=ps[0].rearrange("p (k t) -> p k t", k=4))
                e.copy(out=xv[:, 4:8, tc * 128:tc * 128 + 128],
                       in_=ps[1].rearrange("p (k t) -> p k t", k=4)
                       ).then_inc(sP1c.h, 1)
            on("scalar", p1_s4)
            sP1c.inc()

        if debug and s == 0:
            def dbg_dump(e, w_xn=sP1c.n):
                e.wait_ge(sP1c.h, w_xn)
                e.dma_start(out=xnd_dbg, in_=XN[:]).then_inc(dSp.h, 16)
                e.dma_start(out=st_dbg, in_=stats[:]).then_inc(dSp.h, 16)
                e.dma_start(out=sb_dbg[:, 0:KC], in_=so_sb[:]).then_inc(dSp.h, 16)
                e.dma_start(out=sb_dbg[:, KC:KC + TC], in_=sc_sb[:]).then_inc(dSp.h, 16)
            on("gpsimd", dbg_dump)
            dSp.inc(64)

        qk_mm = {}
        for mc in range(16):
            wsel = wdb if mc % 2 == 0 else eb

            def p3_syw(e, mc=mc, wsel=wsel, w_qk=qk_mm.get(mc - 2, 0),
                       w_wo=tWo.n):
                if w_qk:
                    e.wait_ge(tQK.h, w_qk)
                if w_wo:
                    e.wait_ge(tWo.h, w_wo)
                e.dma_start(out=wsel[:], in_=wqk[mc]).then_inc(dW.h, 16)
            on("sync", p3_syw)
            dW.inc(16)
            n_dw = dW.n

            for nt in range(NT):
                idx = mc * NT + nt
                b = 2 + idx % 2
                w_ps = sP3.n - 1
                w_xn = sP1c.n

                def p3_t(e, nt=nt, b=b, wsel=wsel, n_dw=n_dw, w_ps=w_ps,
                         w_xn=w_xn):
                    e.wait_ge(dW.h, n_dw)
                    e.wait_ge(sP1c.h, w_xn)
                    if w_ps >= 1:
                        e.wait_ge(sP3.h, w_ps)
                    xv = XN.rearrange("p (k t) -> p k t", k=KC)
                    wv_ = wsel.rearrange("p (k c) -> p k c", k=KC)
                    last = None
                    for k in range(KC):
                        last = e.matmul(ps[b][:], wv_[:, k, :],
                                        xv[:, k, nt * 512:(nt + 1) * 512],
                                        start=(k == 0), stop=(k == KC - 1))
                    last.then_inc(tQK.h, 1)
                on("tensor", p3_t)
                tQK.inc()
                ntq = tQK.n
                qk_mm[mc] = tQK.n
                w_stg = dSp.n

                def p3_s(e, ntq=ntq, b=b, w_rp=vR3.n):
                    e.wait_ge(tQK.h, ntq)
                    if w_rp:
                        e.wait_ge(vR3.h, w_rp)
                    e.copy(out=stg[:, 0:512], in_=ps[b][:]).then_inc(sP3.h, 1)
                on("scalar", p3_s)
                sP3.inc()
                ns_ = sP3.n

                def p3_v(e, ns_=ns_, nt=nt, w_stg=w_stg):
                    e.wait_ge(sP3.h, ns_)
                    if w_stg:
                        e.wait_ge(dSp.h, w_stg)
                    cs = slice(nt * 512, (nt + 1) * 512)
                    e.stream_shuffle(out=stg[:, 512:1024], in_=stg[:, 0:512],
                                     mask=[i ^ 1 for i in range(32)])
                    e.tensor_tensor(out=stg[:, 0:512], in0=stg[:, 0:512],
                                    in1=ctab[:, cs], op=ALU.mult)
                    e.tensor_tensor(out=stg[:, 512:1024], in0=stg[:, 512:1024],
                                    in1=stab[:, cs], op=ALU.mult)
                    e.tensor_tensor(out=stg[:, 0:512], in0=stg[:, 0:512],
                                    in1=stg[:, 512:1024], op=ALU.add)
                    e.tensor_copy(out=xqstg[:, 0:512], in_=stg[:, 0:512])
                    e.tensor_tensor(out=xqstg[:, 512:1024],
                                    in0=stg[:, 0:512], in1=xqstg[:, 0:512],
                                    op=ALU.subtract)
                    vguard(e, vR3.h)
                on("vector", p3_v)
                vR3.inc()
                nvr = vR3.n

                def p3_g(e, nvr=nvr, mc=mc, nt=nt):
                    e.wait_ge(vR3.h, nvr)
                    dd = qd if mc < 8 else kd
                    e.dma_start(out=dd[0, mc % 8, :, nt * 512:(nt + 1) * 512],
                                in_=xqstg[:, 0:512]).then_inc(dSp.h, 16)
                    e.dma_start(out=dd[1, mc % 8, :, nt * 512:(nt + 1) * 512],
                                in_=xqstg[:, 512:1024]).then_inc(dSp.h, 16)
                on("gpsimd", p3_g)
                dSp.inc(32)

        for half in range(2):
            def p3b_sy(e, half=half, w_cons=tV.n, w_wo=tWo.n, w_tb=tTB.n):
                if w_cons:
                    e.wait_ge(tV.h, w_cons)
                if w_wo:
                    e.wait_ge(tWo.h, w_wo)
                e.wait_ge(tTB.h, w_tb)
                e.dma_start(out=WBv[:], in_=wvt[half]).then_inc(dW.h, 16)
            on("sync", p3b_sy)
            dW.inc(16)
            n_dwv = dW.n

            for tc in range(TC):
                idx2 = half * TC + tc
                b = idx2 % 2
                w_ev = sV3.n - 1

                def p3b_t(e, tc=tc, b=b, n_dwv=n_dwv, w_ev=w_ev, w_xn=sP1c.n):
                    e.wait_ge(dW.h, n_dwv)
                    e.wait_ge(sP1c.h, w_xn)
                    if w_ev >= 1:
                        e.wait_ge(sV3.h, w_ev)
                    xv = XN.rearrange("p (k t) -> p k t", k=KC)
                    wvv = WBv.rearrange("p (k c) -> p k c", k=KC)
                    last = None
                    for k in range(KC):
                        last = e.matmul(ps[b][:],
                                        xv[:, k, tc * 128:tc * 128 + 128],
                                        wvv[:, k, :],
                                        start=(k == 0), stop=(k == KC - 1))
                    last.then_inc(tV.h, 1)
                on("tensor", p3b_t)
                tV.inc()
                ntv = tV.n
                w_f = dSp.n

                def p3b_s(e, ntv=ntv, b=b, w_f=w_f):
                    e.wait_ge(tV.h, ntv)
                    if w_f:
                        e.wait_ge(dSp.h, w_f)
                    e.copy(out=fstg[:, b * 512:(b + 1) * 512],
                           in_=ps[b][:]).then_inc(sV3.h, 1)
                on("scalar", p3b_s)
                sV3.inc()
                nsv = sV3.n

                def p3b_g(e, nsv=nsv, tc=tc, half=half, b=b):
                    e.wait_ge(sV3.h, nsv)
                    e.dma_start(out=vd[:, tc, half * 512:(half + 1) * 512],
                                in_=fstg[:, b * 512:(b + 1) * 512]
                                ).then_inc(dSp.h, 16)
                on("gpsimd", p3b_g)
                dSp.inc(16)

        allcells = []
        for h in range(NH):
            for qt in range(NT):
                nkc = 4 * qt + 4
                for kc in range(nkc):
                    diag = kc >= 4 * qt
                    j = (kc - 4 * qt) * 128 if diag else 0
                    allcells.append((h, qt, kc, diag, j, kc == 0,
                                     kc == nkc - 1,
                                     qt == NT - 1 and kc == nkc - 1))

        def hbuf(h):
            if h % 2 == 0:
                return (qha, qlb, khl, vh)
            return (qha2, qlb2, khl2, vh2)

        def emit_batch(h, waits):
            def p4_sy(e, h=h, waits=waits):
                for sh, cnt in waits:
                    if cnt:
                        e.wait_ge(sh, cnt)
                qa, ql, kb, vb = hbuf(h)
                r0 = (h % 2) * 64
                src_q = qd[:, h // 2, r0:r0 + 64, :]
                src_k = kd[:, h // 2, r0:r0 + 64, :]
                e.dma_start(out=qa[0:64, :], in_=src_q[0]).then_inc(dH4.h, 16)
                e.dma_start(out=qa[64:128, :], in_=src_q[0]).then_inc(dH4.h, 16)
                e.dma_start(out=ql[0:64, :], in_=src_q[1]).then_inc(dH4.h, 16)
                e.dma_start(out=kb[0:64, :], in_=src_k[0]).then_inc(dH4.h, 16)
                e.dma_start(out=kb[64:128, :], in_=src_k[1]).then_inc(dH4.h, 16)
                vv = vb.rearrange("p (t c) -> p t c", t=TC)
                e.dma_start(out=vv[:, 0:8, :],
                            in_=vd[:, 0:8, h * 64:h * 64 + 64]
                            ).then_inc(dH4.h, 16)
                e.dma_start(out=vv[:, 8:16, :],
                            in_=vd[:, 8:16, h * 64:h * 64 + 64]
                            ).then_inc(dH4.h, 16)
            on("sync", p4_sy)
            dH4.inc(112)
            return dH4.n

        w_sp = dSp.n
        w_b01 = [(dSp.h, w_sp), (tFf.h, tFf.n), (tTr.h, tTr.n),
                 (vO.h, vO.n), (dOut.h, dOut.n)]

        def p4_vz(e, w_ff=tFf.n):
            if w_ff:
                e.wait_ge(tFf.h, w_ff)
            e.memset(qlb2[64:128, :], 0.0)
            e.tensor_tensor(out=dumw[:], in0=dumw[:], in1=dumw[:],
                            op=ALU.max).then_inc(dH4.h, 16)
        on("vector", p4_vz)
        dH4.inc(16)
        ndh = {0: emit_batch(0, w_b01), 1: emit_batch(1, w_b01)}

        def emit_ta(a):
            (ci_, h_, qt_, kc_, diag_, j_, first_, last_, eslot_, nvm_,
             w_se_) = a
            w_pz = sZ.n

            def p4_ta(e, h=h_, kc=kc_, j=j_, eslot=eslot_, w_se=w_se_,
                      nvm=nvm_, diag=diag_, first=first_, last_kc=last_,
                      w_pz=w_pz):
                e.wait_ge(sExp.h, w_se)
                if diag:
                    e.wait_ge(vMk.h, nvm)
                if first and w_pz:
                    e.wait_ge(sZ.h, w_pz)
                vb = hbuf(h)[3]
                esl = ebsl(eslot, j)
                e.matmul(psY[:, j:512], vb[:, kc * 64:(kc + 1) * 64],
                         esl, start=first, stop=last_kc)
                e.matmul(psZ[:, j:512], ones64[:, 0:64], esl,
                         start=first, stop=last_kc).then_inc(tZ.h, 1)
            on("tensor", p4_ta)
            tZ.inc()

        def emit_qt_tail(qt_, h_):
            ntz = tZ.n
            w_yst = dSp.n
            w_zr = vY.n

            def p4_sz(e, ntz=ntz, w_zr=w_zr):
                e.wait_ge(tZ.h, ntz)
                if w_zr:
                    e.wait_ge(vY.h, w_zr)
                e.copy(out=zsb[:], in_=psZ[:])
                e.copy(out=ystr[:], in_=psY[:]).then_inc(sZ.h, 1)
            on("scalar", p4_sz)
            sZ.inc()
            nsz = sZ.n

            def p4_vy(e, nsz=nsz, w_yst=w_yst):
                e.wait_ge(sZ.h, nsz)
                if w_yst:
                    e.wait_ge(dSp.h, w_yst)
                e.reciprocal(out=zsb[:], in_=zsb[:])
                e.tensor_tensor(out=yst[:], in0=ystr[:], in1=zsb[:],
                                op=ALU.mult).then_inc(vY.h, 1)
            on("vector", p4_vy)
            vY.inc()
            nvy = vY.n

            def p4_gy(e, nvy=nvy, h=h_, qt=qt_):
                e.wait_ge(vY.h, nvy)
                e.dma_start(out=yd[(h % 2) * 64:(h % 2) * 64 + 64, h // 2,
                                   qt * 512:(qt + 1) * 512],
                            in_=yst[:]).then_inc(dSp.h, 16)
            on("gpsimd", p4_gy)
            dSp.inc(16)

        LAG = 3
        pend = []
        expcnt = {}
        for ci, (h, qt, kc, diag, j, first, last, hlast) in \
                enumerate(allcells):
            seq_last = (ci == len(allcells) - 1)
            sb = 4 + ci % 2
            eslot = ci % 4
            w_exp = sExp.n - 1

            def p4_ts(e, h=h, qt=qt, kc=kc, j=j, sb=sb, n_dh=ndh[h],
                      w_exp=w_exp):
                e.wait_ge(dH4.h, n_dh)
                if w_exp >= 1:
                    e.wait_ge(sExp.h, w_exp)
                qa, ql, kb, _ = hbuf(h)
                ksl = kb[:, kc * 128:(kc + 1) * 128]
                qsl = slice(qt * 512 + j, (qt + 1) * 512)
                e.matmul(ps[sb][:, j:512], ksl, qa[:, qsl],
                         start=True, stop=False)
                e.matmul(ps[sb][:, j:512], ksl, ql[:, qsl],
                         start=False, stop=True).then_inc(tS.h, 1)
            on("tensor", p4_ts)
            tS.inc()
            nts = tS.n
            w_ec = tZ.n

            def p4_se(e, nts=nts, sb=sb, j=j, eslot=eslot, w_ec=w_ec,
                      seq_last=seq_last):
                e.wait_ge(tS.h, nts)
                if w_ec >= 1:
                    e.wait_ge(tZ.h, w_ec)
                act = e.activation(
                    out=ebsl(eslot, j),
                    in_=ps[sb][:, j:512], func=AF.Exp, scale=0.125,
                    bias=cst[:, 3:4])
                if seq_last:
                    sguard(e, sExp.h)
                else:
                    act.then_inc(sExp.h, 1)
            on("scalar", p4_se)
            sExp.inc()
            expcnt[ci] = sExp.n
            nse = sExp.n

            if diag:
                def p4_vm(e, nse=nse, eslot=eslot, j=j):
                    e.wait_ge(sExp.h, nse)
                    sl = ebsl(eslot, j, j + 128)
                    e.tensor_tensor(out=sl, in0=sl, in1=trimask[:],
                                    op=ALU.mult)
                    vguard(e, vMk.h)
                on("vector", p4_vm)
                vMk.inc()
            nvm = vMk.n

            pend.append((ci, h, qt, kc, diag, j, first, last, eslot, nvm,
                         None))
            if len(pend) > LAG:
                a = pend.pop(0)
                emit_ta(a[:10] + (expcnt[a[0] + 1],))
                if a[7]:
                    emit_qt_tail(a[2], a[1])
                    if allcells[a[0]][7]:
                        hh = a[1]
                        if hh + 2 < NH:
                            ndh[hh + 2] = emit_batch(
                                hh + 2, [(tZ.h, tZ.n)])

        while pend:
            a = pend.pop(0)
            w_se = expcnt.get(a[0] + 1, sExp.n)
            emit_ta(a[:10] + (w_se,))
            if a[7]:
                emit_qt_tail(a[2], a[1])

        wo_mm = {}
        for qt in range(NT):
            def p6_sy(e, qt=qt, w_y=dSp.n, w_wb=tWo.n, w_tv=tV.n):
                e.wait_ge(dSp.h, w_y)
                e.wait_ge(tV.h, w_tv)
                if w_wb:
                    e.wait_ge(tWo.h, w_wb)
                wbv = WBv.rearrange("p (k c) -> p k c", k=KC)
                for k in range(KC):
                    e.dma_start(out=wbv[:, k, :],
                                in_=yd[:, k, qt * 512:(qt + 1) * 512]
                                ).then_inc(dY6.h, 16)
            on("sync", p6_sy)
            dY6.inc(16 * KC)
            n_dy = dY6.n

            for mc in range(KC):
                idx = qt * KC + mc
                wsel = wdb if idx % 2 == 0 else eb

                def p6_syw(e, mc=mc, wsel=wsel,
                           w_free=wo_mm.get(idx - 2, 0), w_qk=tQK.n,
                           w_ta=(tZ.n if idx < 2 else 0)):
                    e.wait_ge(tQK.h, w_qk)
                    if w_ta:
                        e.wait_ge(tZ.h, w_ta)
                    if w_free:
                        e.wait_ge(tWo.h, w_free)
                    e.dma_start(out=wsel[:], in_=wot[mc]).then_inc(dW.h, 16)
                on("sync", p6_syw)
                dW.inc(16)
                n_dwo = dW.n
                b = idx % 2
                w_ev = sWo.n - 1

                def p6_t(e, b=b, wsel=wsel, n_dwo=n_dwo, n_dy=n_dy,
                         w_ev=w_ev):
                    e.wait_ge(dW.h, n_dwo)
                    e.wait_ge(dY6.h, n_dy)
                    if w_ev >= 1:
                        e.wait_ge(sWo.h, w_ev)
                    wv_ = wsel.rearrange("p (k c) -> p k c", k=KC)
                    wbv = WBv.rearrange("p (k c) -> p k c", k=KC)
                    last = None
                    for k in range(KC):
                        last = e.matmul(ps[b][:], wv_[:, k, :], wbv[:, k, :],
                                        start=(k == 0), stop=(k == KC - 1))
                    last.then_inc(tWo.h, 1)
                on("tensor", p6_t)
                tWo.inc()
                wo_mm[idx] = tWo.n
                ntw = tWo.n
                w_f = dAF.n

                def p6_s(e, ntw=ntw, b=b, w_f=w_f, idx=idx, w_dsp=dSp.n):
                    e.wait_ge(tWo.h, ntw)
                    if idx < 2:
                        e.wait_ge(dSp.h, w_dsp)
                    if w_f:
                        e.wait_ge(dAF.h, w_f)
                    e.copy(out=fstg[:, b * 512:(b + 1) * 512],
                           in_=ps[b][:]).then_inc(sWo.h, 1)
                on("scalar", p6_s)
                sWo.inc()
                nsw = sWo.n

                def p6_g(e, nsw=nsw, mc=mc, qt=qt, b=b):
                    e.wait_ge(sWo.h, nsw)
                    e.dma_start(out=ad[:, mc, qt * 512:(qt + 1) * 512],
                                in_=fstg[:, b * 512:(b + 1) * 512]
                                ).then_inc(dAF.h, 16)
                on("gpsimd", p6_g)
                dAF.inc(16)

        def p7_sy(e, w_v=vTB.n, w_hl=tH.n):
            if w_hl:
                e.wait_ge(tH.h, w_hl)
            e.wait_ge(vTB.h, w_v)
            yv = YB.rearrange("p (a k c) -> p a k c", a=2, k=KC)
            e.dma_start(out=yv[:, 0], in_=hmt[0]).then_inc(dHM.h, 16)
            e.dma_start(out=yv[:, 1], in_=hmt[1]).then_inc(dHM.h, 16)
        on("sync", p7_sy)
        dHM.inc(32)
        n_hm = dHM.n

        def emit_t2(tc7, nvl):
            w_ev2 = sXq.n

            def p7_t2(e, nvl=nvl, w_ev2=w_ev2):
                e.wait_ge(vL7.h, nvl)
                if w_ev2 >= 1:
                    e.wait_ge(sXq.h, w_ev2)
                last = None
                for j in range(KC):
                    dst = ps[j // 4][:, (j % 4) * 128:(j % 4) * 128 + 128]
                    last = e.matmul(dst, xqtm[:, j * 128:(j + 1) * 128],
                                    identb[:], start=True, stop=True)
                last.then_inc(tXq.h, 1)
            on("tensor", p7_t2)
            tXq.inc()
            ntx = tXq.n
            w_xs = dXQ.n

            def p7_s2(e, ntx=ntx, w_xs=w_xs):
                e.wait_ge(tXq.h, ntx)
                if w_xs:
                    e.wait_ge(dXQ.h, w_xs)
                e.copy(out=xqstg[:, 0:512], in_=ps[0][:])
                e.copy(out=xqstg[:, 512:1024], in_=ps[1][:]).then_inc(sXq.h, 1)
            on("scalar", p7_s2)
            sXq.inc()
            nsx = sXq.n

            def p7_g(e, nsx=nsx, tc=tc7):
                e.wait_ge(sXq.h, nsx)
                xa = xqstg[:, 0:512].rearrange("p (k c) -> p k c", k=4)
                xb_ = xqstg[:, 512:1024].rearrange("p (k c) -> p k c", k=4)
                e.dma_start(out=xqd[:, 0:4, tc * 128:tc * 128 + 128],
                            in_=xa).then_inc(dXQ.h, 16)
                e.dma_start(out=xqd[:, 4:8, tc * 128:tc * 128 + 128],
                            in_=xb_).then_inc(dXQ.h, 16)
            on("gpsimd", p7_g)
            dXQ.inc(32)

        pend7 = None
        for tc in range(TC):
            pa = 2 + (tc % 2) * 2
            w_bank = sQ7.n - 1
            sx = 4 + (tc % 2) * 6

            def p7_t(e, tc=tc, pa=pa, n_hm=n_hm, w_bank=w_bank, w_xn=sP1c.n):
                e.wait_ge(dHM.h, n_hm)
                e.wait_ge(sP1c.h, w_xn)
                if w_bank >= 1:
                    e.wait_ge(sQ7.h, w_bank)
                xv = XN.rearrange("p (k t) -> p k t", k=KC)
                yv = YB.rearrange("p (a k c) -> p a k c", a=2, k=KC)
                last = None
                for half in range(2):
                    for k in range(KC):
                        last = e.matmul(ps[pa + half][:],
                                        xv[:, k, tc * 128:tc * 128 + 128],
                                        yv[:, half, k, :],
                                        start=(k == 0), stop=(k == KC - 1))
                last.then_inc(tH.h, 1)
            on("tensor", p7_t)
            tH.inc()
            nth = tH.n

            if pend7 is not None:
                emit_t2(*pend7)
                pend7 = None

            def p7_v1(e, nth=nth, pa=pa, sx=sx, tc=tc):
                e.wait_ge(tH.h, nth)
                e.tensor_reduce(out=stats[:, 4 * sx:4 * sx + 1], in_=ps[pa][:],
                                axis=AX.X, op=ALU.max,
                                apply_absolute_value=True)
                i0 = e.tensor_reduce(out=stats[:, 4 * sx + 4:4 * sx + 5],
                                     in_=ps[pa + 1][:], axis=AX.X, op=ALU.max,
                                     apply_absolute_value=True)
                vbar(e, i0)
                i1 = e.tensor_tensor(out=stats[:, 4 * sx:4 * sx + 1],
                                     in0=stats[:, 4 * sx:4 * sx + 1],
                                     in1=stats[:, 4 * sx + 4:4 * sx + 5],
                                     op=ALU.max)
                vbar(e, i1)
                i2 = e.tensor_tensor(out=stats[:, 4 * sx:4 * sx + 1],
                                     in0=stats[:, 4 * sx:4 * sx + 1],
                                     in1=cst[:, 6:7], op=ALU.max)
                vbar(e, i2)
                i3 = e.reciprocal(out=stats[:, 4 * sx + 8:4 * sx + 9],
                                  in_=stats[:, 4 * sx:4 * sx + 1])
                vbar(e, i3)
                i4 = e.tensor_tensor(out=stats[:, 4 * sx + 8:4 * sx + 9],
                                     in0=stats[:, 4 * sx + 8:4 * sx + 9],
                                     in1=cst[:, 10:11], op=ALU.mult)
                vbar(e, i4)
                e.reciprocal(out=sc_sb[:, tc:tc + 1],
                             in_=stats[:, 4 * sx + 8:4 * sx + 9])
                vguard(e, vQ7.h)
            on("vector", p7_v1)
            vQ7.inc()
            nvq = vQ7.n
            w_lad = vL7.n

            def p7_s1(e, nvq=nvq, pa=pa, sx=sx, w_lad=w_lad):
                e.wait_ge(vQ7.h, nvq)
                if w_lad:
                    e.wait_ge(vL7.h, w_lad)
                e.activation(out=scr1[:, 0:512], in_=ps[pa][:], func=AF.Identity,
                             scale=stats[:, 4 * sx + 8:4 * sx + 9], bias=cst[:, 4:5])
                e.activation(out=scr1[:, 512:1024], in_=ps[pa + 1][:],
                             func=AF.Identity, scale=stats[:, 4 * sx + 8:4 * sx + 9],
                             bias=cst[:, 4:5]).then_inc(sQ7.h, 1)
            on("scalar", p7_s1)
            sQ7.inc()
            nsq = sQ7.n
            w_xq = tXq.n

            def p7_v2(e, nsq=nsq, w_xq=w_xq, w_do=dOut.n):
                e.wait_ge(sQ7.h, nsq)
                if w_xq:
                    e.wait_ge(tXq.h, w_xq)
                if w_do:
                    e.wait_ge(dOut.h, w_do)
                e.tensor_copy(out=scr2[:], in_=scr1[:])
                for lv in (8.0, 4.0, 2.0, 1.0):
                    e.tensor_scalar(out=obuf[:], in0=scr2[:], scalar1=lv,
                                    scalar2=-lv, op0=ALU.is_ge, op1=ALU.mult)
                    e.tensor_tensor(out=scr2[:], in0=scr2[:], in1=obuf[:],
                                    op=ALU.add)
                e.scalar_tensor_tensor(out=xqtm[:], in0=scr2[:], scalar=8.0,
                                       in1=scr1[:], op0=ALU.add,
                                       op1=ALU.subtract).then_inc(vL7.h, 1)
            on("vector", p7_v2)
            vL7.inc()
            pend7 = (tc, vL7.n)
        emit_t2(*pend7)
        pend7 = None

        for qt in range(NT):
            def p7b_syx(e, qt=qt, w_xq=dXQ.n, w_cons=tFf.n):
                e.wait_ge(dXQ.h, w_xq)
                if w_cons:
                    e.wait_ge(tFf.h, w_cons)
                e.dma_start(out=xqb.rearrange("p (k c) -> p k c", k=KC),
                            in_=xqd[:, 0:KC, qt * 512:(qt + 1) * 512]
                            ).then_inc(dXB.h, 16)
            on("sync", p7b_syx)
            dXB.inc(16)
            n_xb = dXB.n

            for mc in range(KC):
                idx = qt * KC + mc
                slot = idx % 2

                def p7b_syn(e, mc=mc, slot=slot, w_tn=dTN.n,
                            w_cons=tFf.n - 1,
                            w_ts=(tS.n if idx < 2 else 0)):
                    e.wait_ge(dTN.h, w_tn)
                    if w_ts:
                        e.wait_ge(tS.h, w_ts)
                    if w_cons >= 1:
                        e.wait_ge(tFf.h, w_cons)
                    dst = ntb[:, slot * KC * 128:(slot + 1) * KC * 128]
                    e.dma_start(out=dst.rearrange("p (k c) -> p k c", k=KC),
                                in_=ternd[:, 0:KC, mc * 128:mc * 128 + 128]
                                ).then_inc(dNT.h, 16)
                on("sync", p7b_syn)
                dNT.inc(16)
                n_nt = dNT.n

                def p7b_rly(e, n_nt=n_nt, n_xb=n_xb):
                    e.wait_ge(dNT.h, n_nt)
                    e.wait_ge(dXB.h, n_xb)
                    e.tensor_tensor(out=dumw[:], in0=dumw[:], in1=dumw[:],
                                    op=ALU.max).then_inc(dRL.h, 1)
                on("vector", p7b_rly)
                dRL.inc()
                n_rl = dRL.n
                b = idx % 2
                w_ev = sFf.n - 1

                def p7b_t(e, b=b, slot=slot, n_rl=n_rl,
                          w_ev=w_ev, idx=idx, w_ff0=(sXq.n, sWo.n)):
                    e.wait_ge(dRL.h, n_rl)
                    if w_ev >= 1:
                        e.wait_ge(sFf.h, w_ev)
                    if idx % (NT * KC) < 2:
                        e.wait_ge(sXq.h, w_ff0[0])
                    ntv = ntb[:, slot * KC * 128:(slot + 1) * KC * 128
                              ].rearrange("p (k c) -> p k c", k=KC)
                    xqv = xqb.rearrange("p (k c) -> p k c", k=KC)
                    last = None
                    for k in range(KC):
                        last = e.matmul(ps[b][:], ntv[:, k, :], xqv[:, k, :],
                                        start=(k == 0), stop=(k == KC - 1))
                    last.then_inc(tFf.h, 1)
                on("tensor", p7b_t)
                tFf.inc()
                ntf = tFf.n
                w_f = dAF.n

                def p7b_s(e, ntf=ntf, b=b, mc=mc, w_f=w_f):
                    e.wait_ge(tFf.h, ntf)
                    if w_f:
                        e.wait_ge(dAF.h, w_f)
                    e.activation(out=fstg[:, b * 512:(b + 1) * 512],
                                 in_=ps[b][:], func=AF.Identity,
                                 scale=so_sb[:, mc:mc + 1]).then_inc(sFf.h, 1)
                on("scalar", p7b_s)
                sFf.inc()
                nsf = sFf.n

                def p7b_g(e, nsf=nsf, mc=mc, qt=qt, b=b):
                    e.wait_ge(sFf.h, nsf)
                    e.dma_start(out=fd[:, mc, qt * 512:(qt + 1) * 512],
                                in_=fstg[:, b * 512:(b + 1) * 512]
                                ).then_inc(dAF.h, 16)
                on("gpsimd", p7b_g)
                dAF.inc(16)

        p8_tt = {}
        p8_vv = {}
        p8_gg = {}
        for tc in range(TC):
            xd_ = xdb if tc % 2 == 0 else vh
            s1_ = scr1 if tc % 2 == 0 else XN[:, 0:1024]
            s2_ = scr2 if tc % 2 == 0 else XN[:, 1024:2048]
            stg_ = stg if tc % 2 == 0 else XN[:, 2048:3072]
            fa_ = fstg[:, 0:512] if tc % 2 == 0 else XN[:, 3072:3584]
            fb_ = fstg[:, 512:1024] if tc % 2 == 0 else XN[:, 3584:4096]

            def p8_sy(e, tc=tc, s=s, xd_=xd_, s1_=s1_, s2_=s2_,
                      w_cons=p8_vv.get(tc - 2, 0), w_all=dAF.n,
                      w_tr=(p8_tt.get(tc - 2, 0) if tc >= 2 else tTr.n),
                      w_do=(p8_gg.get(tc - 2, 0) if tc >= 2 else dOut.n),
                      row0=row0, w_sp1=sP1b.n):
                e.wait_ge(dAF.h, w_all)
                e.wait_ge(sP1b.h, w_sp1)
                if w_cons >= 1:
                    e.wait_ge(vO.h, w_cons)
                e.dma_start(out=xd_[:],
                            in_=xin[row0 + tc * 128: row0 + tc * 128 + 128, :]
                            ).then_inc(dX.h, 16)
                if w_tr:
                    e.wait_ge(tTr.h, w_tr)
                if w_do:
                    e.wait_ge(dOut.h, w_do)
                e.dma_start(out=s1_[:],
                            in_=ad[:, 0:KC, tc * 128:tc * 128 + 128]
                            ).then_inc(dX.h, 16)
                e.dma_start(out=s2_[:],
                            in_=fd[:, 0:KC, tc * 128:tc * 128 + 128]
                            ).then_inc(dX.h, 16)
            on("sync", p8_sy)
            dX.inc(48)
            n_d8 = dX.n
            w_ps = sTr.n

            def p8_rly(e, n_d8=n_d8):
                e.wait_ge(dX.h, n_d8)
                e.tensor_tensor(out=dumw[:], in0=dumw[:], in1=dumw[:],
                                op=ALU.max).then_inc(dRL.h, 1)
            on("vector", p8_rly)
            dRL.inc()
            n_rl8 = dRL.n

            def p8_t(e, n_rl8=n_rl8, s1_=s1_, s2_=s2_, w_ps=w_ps,
                     w_ff=sFf.n):
                e.wait_ge(dRL.h, n_rl8)
                e.wait_ge(sFf.h, w_ff)
                if w_ps:
                    e.wait_ge(sTr.h, w_ps)
                last = None
                for j in range(KC):
                    dst = ps[j // 4][:, (j % 4) * 128:(j % 4) * 128 + 128]
                    last = e.matmul(dst, s1_[:, j * 128:(j + 1) * 128],
                                    identf[:], start=True, stop=True)
                for j in range(KC):
                    dst = ps[2 + j // 4][:, (j % 4) * 128:(j % 4) * 128 + 128]
                    last = e.matmul(dst, s2_[:, j * 128:(j + 1) * 128],
                                    identf[:], start=True, stop=True)
                last.then_inc(tTr.h, 1)
            on("tensor", p8_t)
            tTr.inc()
            ntt = tTr.n
            p8_tt[tc] = tTr.n

            def p8_s(e, ntt=ntt, tc=tc, stg_=stg_, fa_=fa_, fb_=fb_,
                     w_dsp=dSp.n, w_vp=p8_vv.get(tc - 2, 0)):
                e.wait_ge(tTr.h, ntt)
                if tc == 0:
                    e.wait_ge(dSp.h, w_dsp)
                if w_vp:
                    e.wait_ge(vO.h, w_vp)
                e.copy(out=stg_[:, 0:512], in_=ps[0][:])
                e.copy(out=stg_[:, 512:1024], in_=ps[1][:])
                e.activation(out=fa_, in_=ps[2][:],
                             func=AF.Identity, scale=sc_sb[:, tc:tc + 1])
                e.activation(out=fb_, in_=ps[3][:],
                             func=AF.Identity,
                             scale=sc_sb[:, tc:tc + 1]).then_inc(sTr.h, 1)
            on("scalar", p8_s)
            sTr.inc()
            nst = sTr.n

            def p8_v(e, nst=nst, tc=tc, xd_=xd_, s1_=s1_, stg_=stg_,
                     fa_=fa_, fb_=fb_):
                e.wait_ge(sTr.h, nst)
                e.tensor_tensor(out=s1_[:, 0:512], in0=stg_[:, 0:512],
                                in1=fa_, op=ALU.add)
                e.tensor_tensor(out=s1_[:, 512:1024], in0=stg_[:, 512:1024],
                                in1=fb_, op=ALU.add)
                e.tensor_tensor(out=s1_[:], in0=s1_[:], in1=xd_[:],
                                op=ALU.add).then_inc(vO.h, 1)
            on("vector", p8_v)
            vO.inc()
            nvo = vO.n
            p8_vv[tc] = vO.n

            def p8_g(e, nvo=nvo, tc=tc, s1_=s1_, row0=row0):
                e.wait_ge(vO.h, nvo)
                e.dma_start(out=out[row0 + tc * 128: row0 + tc * 128 + 128, :],
                            in_=s1_[:]).then_inc(dOut.h, 16)
            on("gpsimd", p8_g)
            dOut.inc(16)
            p8_gg[tc] = dOut.n

    def fin(e, w=dOut.n):
        e.wait_ge(dOut.h, w)
    on("gpsimd", fin)

    with nc.Block() as block:
        for eng in ("sync", "gpsimd", "tensor", "vector", "scalar"):
            fns = ops[eng]

            def make(fns=fns):
                def section(e):
                    for fn in fns:
                        fn(e)
                return section
            getattr(block, eng)(make())

    return nc


def _hadamard(size):
    h = np.array([[1.0]], dtype=np.float64)
    while h.shape[0] < size:
        h = np.block([[h, h], [h, -h]])
    return (h / np.sqrt(size)).astype(np.float32)


def _host_inputs(L0_attn_W_qkv, L0_attn_W_o, L0_ffn_weight):
    Wqkv = np.ascontiguousarray(np.asarray(L0_attn_W_qkv, dtype=np.float32))
    Wo = np.ascontiguousarray(np.asarray(L0_attn_W_o, dtype=np.float32))
    Wf = np.ascontiguousarray(np.asarray(L0_ffn_weight, dtype=np.float32))

    perm64 = np.empty(64, np.int64)
    perm64[0::2] = np.arange(32)
    perm64[1::2] = np.arange(32, 64)
    colperm = np.arange(3072)
    for h in range(NH):
        colperm[64 * h:64 * h + 64] = 64 * h + perm64
        colperm[1024 + 64 * h:1024 + 64 * h + 64] = 1024 + 64 * h + perm64
    Wp = Wqkv[:, colperm]

    wqk = np.ascontiguousarray(
        Wp[:, :2048].reshape(KC, 128, 16, 128).transpose(2, 1, 0, 3))
    wvt = np.ascontiguousarray(
        Wp[:, 2048:].reshape(KC, 128, 2, 512).transpose(2, 1, 0, 3))
    wot = np.ascontiguousarray(
        Wo.reshape(KC, 128, KC, 128).transpose(2, 1, 0, 3))
    wft = np.ascontiguousarray(Wf.reshape(KC, 128, D))
    H = _hadamard(D)
    hmt = np.ascontiguousarray(
        H.reshape(KC, 128, 2, 512).transpose(2, 1, 0, 3))
    iota = np.ascontiguousarray(
        np.broadcast_to(np.arange(T, dtype=np.float32), (128, T)))
    f = (np.arange(128) % 64) // 2
    theta = 1.0 / (10000.0 ** (f.astype(np.float64) / 32.0))
    th = theta.astype(np.float32)
    tl = (theta - th.astype(np.float64)).astype(np.float32)
    csf = np.where(np.arange(128) % 2 == 0, np.float32(-1.0), np.float32(1.0))
    ones = np.ones(128, np.float32)
    cols = [th, tl, csf.astype(np.float32),
            -12.0 * ones, 8.5 * ones,
            np.float32(2.0 ** -10) * ones, np.float32(1e-5) * ones,
            0.5 * ones, -1.0 * ones,
            np.float32(np.float32(1.0) / np.float32(7.0)) * ones,
            7.0 * ones, 0.0 * ones]
    thet = np.ascontiguousarray(np.stack(cols, axis=1).astype(np.float32))
    return wqk, wvt, wot, wft, hmt, iota, thet


def kernel(L0_attn_x, L0_attn_W_qkv, L0_attn_W_o, L0_ffn_weight):
    x = np.ascontiguousarray(np.asarray(L0_attn_x, dtype=np.float32))
    wqk, wvt, wot, wft, hmt, iota, thet = _host_inputs(
        L0_attn_W_qkv, L0_attn_W_o, L0_ffn_weight)

    nc = build_nc()
    in_maps = []
    for c in range(8):
        xs = np.ascontiguousarray(x[2 * c:2 * c + 2].reshape(NTOK, D))
        in_maps.append({"xin": xs, "wqk": wqk, "wvt": wvt, "wot": wot,
                        "wft": wft, "hmt": hmt, "iota": iota, "thet": thet})
    res = run_bass_kernel_spmd(nc, in_maps, core_ids=list(range(8)))
    outs = [np.asarray(r["out"]) for r in res.results]
    full = np.stack(outs, axis=0).reshape(16, T, D)
    return full
